# version 19
# baseline (speedup 1.0000x reference)
import math
import numpy as np
import ml_dtypes
import concourse.bass as bass
import concourse.mybir as mybir
from concourse.bass_utils import run_bass_kernel_spmd

F32 = mybir.dt.float32
BF16 = mybir.dt.bfloat16
I32 = mybir.dt.int32
AF = mybir.ActivationFunctionType
ALU = mybir.AluOpType
AX = mybir.AxisListType

QMAX = 127.0
EPS = 1e-5
ENGS = ("sync", "gpsimd", "scalar", "vector", "tensor")


class Emitter:

    def __init__(self, nc, sem_names):
        self.nc = nc
        self.streams = {k: [] for k in ENGS}
        self.h = {n: nc.alloc_semaphore("s_" + n) for n in sem_names}
        self.n = {n: 0 for n in sem_names}

    def now(self, sem):
        return self.n[sem]

    def op(self, eng, fn, waits=(), inc=None, k=1, self_inc=False):
        rw = []
        for nm, v in waits:
            assert v <= self.n[nm], ("wait ahead of emission", nm, v, self.n[nm])
            if v > 0:
                rw.append((self.h[nm], v))
        ih = None
        if inc is not None:
            self.n[inc] += k
            ih = self.h[inc]
        self.streams[eng].append((rw, fn, ih, k, self_inc))
        return self.n[inc] if inc is not None else None

    def replay(self, eng_name, e):
        for rw, fn, ih, k, self_inc in self.streams[eng_name]:
            for hh, v in rw:
                e.wait_ge(hh, v)
            if self_inc:
                fn(e, ih)
            else:
                ins = fn(e)
                if ih is not None:
                    assert ins is not None
                    ins.then_inc(ih, k)


def build_nc(S=8192, D=1024, DI=2816):
    NB = S // 128
    NU = S // 512
    NT = S // 512
    NFB = S // 1024
    NF = S // 1024
    NF2 = S // 512
    NC = D // 128
    NI = DI // 128
    SM = S // 2

    nc = bass.Bass("TRN2", target_bir_lowering=False)

    xt = nc.dram_tensor("x_true", [S, D], F32, kind="ExternalInput").ap()
    xtf = nc.dram_tensor("x_true_fm", [D, S], F32, kind="ExternalInput").ap()
    xm = nc.dram_tensor("x_mine", [SM, D], F32, kind="ExternalInput").ap()
    wq_h = nc.dram_tensor("wq", [D, D], F32, kind="ExternalInput").ap()
    wk_h = nc.dram_tensor("wk", [D, D], F32, kind="ExternalInput").ap()
    wv_h = nc.dram_tensor("wv", [D, D], F32, kind="ExternalInput").ap()
    wo_h = nc.dram_tensor("wo", [D, D], F32, kind="ExternalInput").ap()
    b_h = {n: nc.dram_tensor(n, [1, D], F32, kind="ExternalInput").ap()
           for n in ("bq", "bk", "bv", "bo")}
    cx_h = nc.dram_tensor("clip_x", [1, 1], F32, kind="ExternalInput").ap()
    cd_h = nc.dram_tensor("clip_down", [1, 1], F32, kind="ExternalInput").ap()
    wg_h = nc.dram_tensor("w_gate", [DI, D], F32, kind="ExternalInput").ap()
    wu_h = nc.dram_tensor("w_up", [DI, D], F32, kind="ExternalInput").ap()
    wd_h = nc.dram_tensor("w_down", [D, DI], F32, kind="ExternalInput").ap()
    msk_h = nc.dram_tensor("masks", [5 * 128, 256], BF16, kind="ExternalInput").ap()
    fix_h = nc.dram_tensor("fixes", [128, 512], BF16, kind="ExternalInput").ap()
    fxl_h = nc.dram_tensor("fix_l", [1, 256], F32, kind="ExternalInput").ap()
    idn_h = nc.dram_tensor("ident", [128, 128], F32, kind="ExternalInput").ap()
    y_h = nc.dram_tensor("y", [SM, D], F32, kind="ExternalOutput").ap()

    xn_sc = nc.dram_tensor("xn_sc", [S, D], BF16, kind="Internal").ap()
    xnm_sc = nc.dram_tensor("xnm_sc", [SM, D], BF16, kind="Internal").ap()
    K_sc = nc.dram_tensor("K_sc", [NC, 128, S], BF16, kind="Internal").ap()
    V_sc = nc.dram_tensor("V_sc", [S, D], BF16, kind="Internal").ap()
    ao_sc = nc.dram_tensor("ao_sc", [D, SM], BF16, kind="Internal").ap()
    t_sc = nc.dram_tensor("t_sc", [NI, 128, SM], F32, kind="Internal").ap()
    wgq_sc = nc.dram_tensor("wgq_sc", [DI, D], BF16, kind="Internal").ap()
    wuq_sc = nc.dram_tensor("wuq_sc", [DI, D], BF16, kind="Internal").ap()
    wdq_sc = nc.dram_tensor("wdq_sc", [D, DI], BF16, kind="Internal").ap()
    ar1_in = nc.dram_tensor("ar1_in", [1, 16], F32, kind="Internal").ap()
    ar1_out = nc.dram_tensor("ar1_out", [1, 16], F32, kind="Internal").ap()
    ar2_in = nc.dram_tensor("ar2_in", [1, 16], F32, kind="Internal").ap()
    ar2_out = nc.dram_tensor("ar2_out", [1, 16], F32, kind="Internal").ap()

    A = nc.alloc_sbuf_tensor
    msk_sb = A("msk_sb", [128, 5 * 256], BF16)
    fix_sb = A("fix_sb", [128, 512], BF16)
    fxl_sb = A("fxl_sb", [1, 256], F32)
    idn_sb = A("idn_sb", [128, 128], F32)
    ones_c = A("ones_c", [128, 1], BF16)
    ones_r = A("ones_r", [1, 128], F32)
    bq_sb = A("bq_sb", [128, NC], F32)
    bk_sb = A("bk_sb", [128, NC], F32)
    bo_sb = A("bo_sb", [128, NC], F32)
    bv_row = A("bv_row", [1, D], F32)
    bv_rowb = A("bv_rowb", [1, D], BF16)
    ones_rb = A("ones_rb", [1, 128], BF16)
    cx_sb = A("cx_sb", [1, 1], F32)
    cd_sb = A("cd_sb", [1, 1], F32)
    gmax_run = A("gmax_run", [128, 1], F32)
    tmax_run = A("tmax_run", [128, 1], F32)
    gstat = A("gstat", [1, 16], F32)
    scl = A("scl", [1, 16], F32)
    sx_bc = A("sx_bc", [128, 1], F32)
    rx_bc = A("rx_bc", [128, 1], F32)
    sh_bc = A("sh_bc", [128, 1], F32)
    rh_bc = A("rh_bc", [128, 1], F32)
    vmean = A("vmean", [128, NC], F32)
    vsum_row = A("vsum_row", [1, D], F32)
    lnst = A("lnst", [128, 12 * 8], F32)
    stats_mr = A("stats_mr", [128, 64], F32)
    lmv = A("lmv", [128, 4 * 8], F32)
    wqs = A("wqs", [128, 4], F32)
    tstat = A("tstat", [128, 1], F32)
    l_raw = A("l_raw", [1, 256], F32)
    eps_sb = A("eps_sb", [128, 1], F32)
    linv_sb = A("linv_sb", [1, 256], F32)

    base0 = nc.sbuf_base
    SB_CAP = 212700
    KB = 1024

    def at(name, shape, dtype, off):
        esz = {F32: 4, BF16: 2, I32: 4}[dtype]
        assert base0 + off + shape[-1] * esz <= SB_CAP, (name, base0, off)
        return nc.alloc_sbuf_tensor_at(name, shape, dtype, offset=base0 + off)

    xA_t = at("xA0", [128, 4 * D], F32, 0)
    xA = [xA_t, xA_t]
    xfr = [at(f"xfr{i}", [128, NC * 512], F32, 16 * KB + i * 16 * KB) for i in range(2)]
    xfm = [at(f"xfm{i}", [128, NC * 512], BF16, 48 * KB + i * 8 * KB) for i in range(2)]
    kstage = [at(f"kstage{i}", [128, NC * 512], BF16, 64 * KB + i * 8 * KB) for i in range(2)]
    vstage = [at(f"vstage{i}", [128, 4 * D], BF16, 80 * KB + i * 8 * KB) for i in range(2)]
    scrm = at("scrm", [128, NC * 512], BF16, 96 * KB)
    scrsq = at("scrsq", [128, NC * 512], BF16, 104 * KB)
    mbc_t = at("mbc", [128, 2 * 512], F32, 112 * KB)
    mbc = [mbc_t, mbc_t]
    mrow_t = at("mrow", [1, 1024], F32, 116 * KB)
    mrow = [mrow_t, mrow_t]
    q_sb = [at(f"q_sb{i}", [128, NC * 256], BF16, i * 4 * KB) for i in range(2)]
    qxn = [at(f"qxn{i}", [128, NC * 256], BF16, 8 * KB + i * 4 * KB) for i in range(2)]
    kst = [at(f"kst{i}", [128, NC * 512], BF16, 16 * KB + i * 8 * KB) for i in range(2)]
    vst = [at(f"vst{i}", [128, 4 * D], BF16, 32 * KB + i * 8 * KB) for i in range(2)]
    exps = at("exps", [128, 4 * 256], BF16, 48 * KB)
    ao_st = [at(f"ao_st{i}", [128, NC * 256], BF16, 50 * KB + i * 4 * KB) for i in range(2)]
    wld = [at(f"wld{i}", [128, DI], F32, 58 * KB + i * 11 * KB) for i in range(2)]
    wqt = at("wqt", [128, DI], F32, 80 * KB)
    wqi = at("wqi", [128, DI], I32, 91 * KB)
    wqo = at("wqo", [128, DI], BF16, 102 * KB)
    onorm = at("onorm", [128, NC * 256], BF16, 107 * KB + 512)
    linv_bc = at("linv_bc", [128, 256], F32, 111 * KB + 512)
    wq_sb = at("wq_sb", [128, NC * D], BF16, 120 * KB)
    wk_sb = at("wk_sb", [128, NC * D], BF16, 136 * KB)
    wo_sb = at("wo_sb", [128, NC * D], BF16, 136 * KB)
    wv_sb = at("wv_sb", [128, NC * D], BF16, 152 * KB)
    wgT = at("wgT", [128, NC * DI], BF16, 0)
    wuT = at("wuT", [128, NC * DI], BF16, 44 * KB)
    FA = 88 * KB
    xq_sb = [at(f"xq_sb{i}", [128, NC * 512], BF16, FA + i * 8 * KB) for i in range(2)]
    g_sb = [at(f"g_sb{i}", [128, 512], F32, FA + 16 * KB + i * 2 * KB) for i in range(2)]
    t_sl = [at(f"t_sl{i}", [128, 512], F32, FA + 20 * KB + i * 2 * KB) for i in range(4)]
    xqt = [at(f"xqt{i}", [128, 512], F32, FA + 28 * KB + i * 2 * KB) for i in range(4)]
    xqi = [at(f"xqi{i}", [128, 512], I32, FA + 36 * KB + i * 2 * KB) for i in range(4)]
    wdT = at("wdT", [128, NI * 1024], BF16, 88 * KB)
    tB = [at(f"tB{i}", [128, NI * 256], F32, i * 22 * KB) for i in range(2)]
    hB = [at(f"hB{i}", [128, NI * 256], BF16, 44 * KB + i * 11 * KB) for i in range(2)]
    tBi = at("tBi", [128, NI * 256], I32, 66 * KB)
    FB = 132 * KB
    xB = [at(f"xB{i}", [128, 2 * D], F32, FB + i * 8 * KB) for i in range(2)]
    aoB = [at(f"aoB{i}", [128, D], BF16, FB + 16 * KB + i * 2 * KB) for i in range(4)]
    outB = [at(f"outB{i}", [128, D], F32, FB + 24 * KB + i * 4 * KB) for i in range(2)]

    P = nc.alloc_psum_tensor
    sc_ps = [P(f"sc{i}", [128, 512], F32) for i in range(2)]
    po_ps = [P(f"po{i}", [128, 512], F32) for i in range(4)]
    pl_ps = P("pl", [1, 1024], F32)

    E = Emitter(nc, [
        "c0", "wl", "wo", "ini",
        "lx0", "lx1", "xfl0", "xfl1", "st", "sq", "xn", "gm", "xc0", "xc1", "xf0", "xf1",
        "kpe", "kev", "vpe", "vev", "vsp", "sqc",
        "kspk0", "kspk1", "kspv0", "kspv1",
        "tp", "vme", "cc1", "cc2", "arx",
        "qx0", "qx1", "qpe", "qev", "kld0", "kld1", "vld0", "vld1",
        "sc", "ex", "mk", "pv", "epi", "bc2", "ope", "ob", "aos0", "aos1",
        "wfl0", "wfl1", "wqd", "wstg", "wstu", "wstd", "wt",
        "xqt0", "xqt1", "xqd", "xqa", "gu", "sil", "tt",
        "tst0", "tst1", "tst2", "tst3",
        "tp2", "sch",
        "tld0", "tld1", "xbl0", "xbl1", "aot0", "aot1", "aot2", "aot3", "hq", "hqa", "dn", "out",
        "yst0", "yst1",
    ])

    inv127 = float(np.float32(1.0) / np.float32(127.0))
    exp_scale = 1.0 / math.sqrt(float(D))

    def consts(e, sh):
        e.dma_start(out=msk_sb[:], in_=msk_h.rearrange("(m p) q -> p m q", p=128)).then_inc(sh, 16)
        e.dma_start(out=fix_sb[:], in_=fix_h[:]).then_inc(sh, 16)
        e.dma_start(out=fxl_sb[:], in_=fxl_h[:]).then_inc(sh, 16)
        e.dma_start(out=idn_sb[:], in_=idn_h[:]).then_inc(sh, 16)
        e.dma_start(out=cx_sb[:], in_=cx_h[:]).then_inc(sh, 16)
        e.dma_start(out=cd_sb[:], in_=cd_h[:]).then_inc(sh, 16)
        e.dma_start(out=bv_row[:], in_=b_h["bv"][:]).then_inc(sh, 16)
    E.op("sync", consts, inc="c0", k=16 * 7, self_inc=True)

    for c in range(NC):
        def bias3(e, sh, c=c):
            e.dma_start(out=bq_sb[:, c:c + 1], in_=b_h["bq"][0:1, c * 128:(c + 1) * 128].rearrange("o p -> p o")).then_inc(sh, 16)
            e.dma_start(out=bk_sb[:, c:c + 1], in_=b_h["bk"][0:1, c * 128:(c + 1) * 128].rearrange("o p -> p o")).then_inc(sh, 16)
            e.dma_start(out=bo_sb[:, c:c + 1], in_=b_h["bo"][0:1, c * 128:(c + 1) * 128].rearrange("o p -> p o")).then_inc(sh, 16)
        E.op("sync", bias3, inc="c0", k=48, self_inc=True)
    C0 = E.now("c0")

    def aw(e, sh):
        for w_sb, w_hh in ((wq_sb, wq_h), (wk_sb, wk_h), (wv_sb, wv_h)):
            e.dma_start(out=w_sb[:], in_=w_hh.rearrange("(c p) n -> p c n", p=128)).then_inc(sh, 16)
    E.op("gpsimd", aw, inc="wl", k=48, self_inc=True)

    nc.const_aps.register(F32, EPS, eps_sb[:, 0:1])

    def inis(e):
        e.memset(eps_sb[:], EPS)
        e.memset(gstat[:], 0.0)
        e.memset(ones_c[:], 1.0)
        e.memset(ones_r[:], 1.0)
        e.memset(ones_rb[:], 1.0)
        e.memset(gmax_run[:], 0.0)
        return e.memset(tmax_run[:], 0.0)
    E.op("gpsimd", inis, inc="ini")

    def bvcp(e):
        return e.tensor_copy(bv_rowb[:], bv_row[:])
    E.op("vector", bvcp, waits=[("c0", C0)], inc="vev")

    kq_free = [("c0", 0), ("c0", 0)]
    po_free = [("c0", 0), ("c0", 0), ("c0", 0), ("c0", 0)]
    mrow_free = [("c0", 0)]
    scrm_free = [("c0", 0), ("c0", 0)]
    scrsq_free = [("c0", 0)]
    rowmath_done = [0]
    sqc_val = [0]
    sq_val = [0]
    xfm_free = {0: 0, 1: 0}
    xfm_built = {0: 0, 1: 0}
    xfr_free = {}
    xA_free = {}
    mbc_free = {0: 0, 1: 0}
    lnst_free = {}
    kstage_free = [("c0", 0), ("c0", 0)]
    vstage_free = [("c0", 0), ("c0", 0)]

    sched = []
    jm_next = 0
    for j in range(NT):
        sched.append(("A", j))
        if j % 2 == 1 and jm_next < NFB:
            sched.append(("B", jm_next))
            jm_next += 1
    while jm_next < NFB:
        sched.append(("B", jm_next))
        jm_next += 1

    def kv_stage(j, p2):
        xfv = xfm_built[p2]

        for o in range(NC):
            bank = o % 2
            E.op("tensor", lambda e, p2=p2, o=o, bank=bank: [
                 e.matmul(sc_ps[bank][:, 0:512],
                          wk_sb[:, c * D + o * 128:c * D + o * 128 + 128],
                          xfm[p2][:, 512 * c:512 * (c + 1)],
                          start=(c == 0), stop=(c == NC - 1)) for c in range(NC)][-1],
                 waits=[("xn", xfv), ("wl", 48), kq_free[bank]], inc="kpe")
            wv_ = [("kpe", E.now("kpe")), ("c0", C0)]
            if o == 0:
                wv_.append(kstage_free[p2])
            E.op("scalar", lambda e, p2=p2, o=o, bank=bank:
                 e.activation(out=kstage[p2][:, 512 * o:512 * (o + 1)], in_=sc_ps[bank][:, 0:512],
                              func=AF.Identity, bias=bk_sb[:, o:o + 1]),
                 waits=wv_, inc="kev")
            kq_free[bank] = ("kev", E.now("kev"))

        for v in range(8):
            tt_, dvb = v // 2, v % 2
            bank = v % 2

            def vmm(e, p2=p2, tt_=tt_, dvb=dvb, bank=bank):
                for c in range(NC):
                    e.matmul(po_ps[bank][:, 0:512],
                             xfm[p2][:, 512 * c + 128 * tt_:512 * c + 128 * tt_ + 128],
                             wv_sb[:, c * D + 512 * dvb:c * D + 512 * dvb + 512],
                             start=(c == 0), stop=False, skip_group_check=True)
                return e.matmul(po_ps[bank][:, 0:512], ones_rb[:],
                                bv_rowb[0:1, 512 * dvb:512 * dvb + 512],
                                start=False, stop=True, skip_group_check=True)
            E.op("tensor", vmm,
                 waits=[("xn", xfv), ("wl", 48), ("vev", 1), po_free[bank]], inc="vpe")
            wv_ = [("vpe", E.now("vpe"))]
            if v == 0:
                wv_.append(vstage_free[p2])
            E.op("scalar", lambda e, p2=p2, tt_=tt_, dvb=dvb, bank=bank:
                 e.activation(out=vstage[p2][:, 1024 * tt_ + 512 * dvb:1024 * tt_ + 512 * dvb + 512],
                              in_=po_ps[bank][:, 0:512], func=AF.Identity),
                 waits=wv_, inc="vev")
            po_free[bank] = ("vev", E.now("vev"))

        def vsum(e, j=j, p2=p2):
            for v in range(8):
                tt_, dvb = v // 2, v % 2
                ins = e.matmul(pl_ps[0:1, 512 * dvb:512 * dvb + 512], ones_c[:],
                               vstage[p2][:, 1024 * tt_ + 512 * dvb:1024 * tt_ + 512 * dvb + 512],
                               start=(j == 0 and v < 2), stop=(j == NT - 1 and v >= 6),
                               skip_group_check=True)
            return ins
        E.op("tensor", vsum, waits=[("vev", E.now("vev")), ("ini", 1)], inc="vsp")
        xfm_free[p2] = E.now("vsp")

        kss = "kspk%d" % p2
        E.op("gpsimd", lambda e, j=j, p2=p2:
             e.dma_start(out=K_sc.rearrange("o p s -> p o s")[:, :, 512 * j:512 * (j + 1)],
                         in_=kstage[p2][:]),
             waits=[("kev", E.now("kev"))], inc=kss, k=16)
        kstage_free[p2] = (kss, E.now(kss))
        vss = "kspv%d" % p2
        E.op("gpsimd", lambda e, j=j, p2=p2:
             e.dma_start(out=V_sc[512 * j:512 * (j + 1), :].rearrange("(k p) d -> p k d", p=128),
                         in_=vstage[p2][:]),
             waits=[("vev", E.now("vev"))], inc=vss, k=16)
        vstage_free[p2] = (vss, E.now(vss))

    xfl_val = {}

    def emit_ld(k):
        p = k % 2
        jj = a_js[k]
        E.op("sync", lambda e, p=p, jj=jj:
             e.dma_start(out=xfr[p][:], in_=xtf[:, 512 * jj:512 * (jj + 1)]
                         .rearrange("(c p) s -> p c s", p=128)),
             waits=xfr_free.get(p, []), inc="xfl%d" % p, k=16)
        xfl_val[p] = E.now("xfl%d" % p)

    def a_scrm(k):
        jj = a_js[k]
        E.op("gpsimd", lambda e, jj=jj:
             e.dma_start(out=scrm[:], in_=xtf[:, 512 * jj:512 * (jj + 1)]
                         .rearrange("(c p) s -> p c s", p=128)),
             waits=list(scrm_free), inc="sqc", k=16)
        sqc_val[0] = E.now("sqc")

    def a_squares(k):
        for c in range(NC):
            w0 = [("sqc", sqc_val[0]), scrsq_free[0]] if c == 0 else []
            E.op("scalar", lambda e, c=c:
                 e.activation(out=scrsq[:, 512 * c:512 * (c + 1)],
                              in_=scrm[:, 512 * c:512 * (c + 1)], func=AF.Square),
                 waits=w0, inc="sq")
        sq_val[0] = E.now("sq")

    def a_sums(k):
        def msum(e):
            for c in range(NC):
                ins = e.matmul(po_ps[2][0:1, 0:512], ones_c[:],
                               scrm[:, 512 * c:512 * (c + 1)],
                               start=(c == 0), stop=(c == NC - 1),
                               skip_group_check=True)
            return ins
        E.op("tensor", msum, waits=[("sqc", sqc_val[0]), po_free[2], ("ini", 1)], inc="tp")
        scrm_free[0] = ("tp", E.now("tp"))
        scrm_free[1] = ("sq", sq_val[0])

        def sqsum(e):
            for c in range(NC):
                ins = e.matmul(po_ps[3][0:1, 0:512], ones_c[:],
                               scrsq[:, 512 * c:512 * (c + 1)],
                               start=(c == 0), stop=(c == NC - 1),
                               skip_group_check=True)
            return ins
        E.op("tensor", sqsum, waits=[("sq", sq_val[0]), po_free[3]], inc="tp")
        scrsq_free[0] = ("tp", E.now("tp"))
        sqsv = E.now("tp")
        p = k % 2
        E.op("vector", lambda e, p=p:
             e.tensor_scalar(mrow[p][0:1, 0:512], po_ps[2][0:1, 0:512],
                             1.0 / float(D), None, ALU.mult),
             waits=[("tp", sqsv), mrow_free[0]], inc="st")
        E.op("vector", lambda e, p=p:
             e.tensor_scalar(mrow[p][0:1, 512:1024], po_ps[3][0:1, 0:512],
                             1.0 / float(D), None, ALU.mult),
             waits=[], inc="st")
        E.op("vector", lambda e, p=p:
             e.tensor_tensor(po_ps[2][0:1, 0:512], mrow[p][0:1, 0:512],
                             mrow[p][0:1, 0:512], ALU.mult),
             waits=[("st", E.now("st"))], inc="st")
        E.op("vector", lambda e, p=p:
             e.tensor_tensor(mrow[p][0:1, 512:1024], mrow[p][0:1, 512:1024],
                             po_ps[2][0:1, 0:512], ALU.subtract),
             waits=[("st", E.now("st"))], inc="st")
        E.op("scalar", lambda e, p=p:
             e.activation(out=po_ps[2][0:1, 0:512], in_=mrow[p][0:1, 512:1024],
                          func=AF.Sqrt, bias=EPS),
             waits=[("st", E.now("st"))], inc="sq")
        E.op("vector", lambda e, p=p:
             e.reciprocal(mrow[p][0:1, 512:1024], po_ps[2][0:1, 0:512]),
             waits=[("sq", E.now("sq"))], inc="st")
        rowmath_done[0] = E.now("st")

    def a_norm(k, j):
        p = k % 2
        s0 = 4 * p

        def bcm2(e, p=p):
            e.matmul(po_ps[2][:, 0:512], ones_r[:], mrow[p][0:1, 0:512],
                     start=True, stop=True)
            return e.matmul(po_ps[3][:, 0:512], ones_r[:], mrow[p][0:1, 512:1024],
                            start=True, stop=True)
        E.op("tensor", bcm2, waits=[("st", rowmath_done[0])], inc="tp")
        mrow_free[0] = ("tp", E.now("tp"))
        E.op("vector", lambda e, p=p:
             e.tensor_copy(mbc[p][:, 0:512], po_ps[2][:, 0:512]),
             waits=[("tp", E.now("tp")), ("xn", max(mbc_free.values()))], inc="st")
        po_free[2] = ("st", E.now("st"))
        E.op("vector", lambda e, p=p:
             e.tensor_copy(mbc[p][:, 512:1024], po_ps[3][:, 0:512]),
             waits=[], inc="st")
        po_free[3] = ("st", E.now("st"))
        bcv = E.now("st")
        for c in range(NC):
            w0 = []
            if c == 0:
                w0 = [("st", bcv), ("vsp", xfm_free.get(p, 0)),
                      ("xfl%d" % p, xfl_val[p])]
            E.op("vector", lambda e, p=p, c=c:
                 e.tensor_tensor(xfr[p][:, 512 * c:512 * (c + 1)],
                                 xfr[p][:, 512 * c:512 * (c + 1)], mbc[p][:, 0:512],
                                 ALU.subtract),
                 waits=w0, inc="xn")
            E.op("vector", lambda e, p=p, c=c:
                 e.tensor_mul(xfm[p][:, 512 * c:512 * (c + 1)],
                              xfr[p][:, 512 * c:512 * (c + 1)], mbc[p][:, 512:1024]),
                 waits=[("xn", E.now("xn"))], inc="xn")
        xfr_free[p] = [("xn", E.now("xn"))]
        mbc_free[p] = E.now("xn")
        E.op("vector", lambda e, p=p, s0=s0:
             e.reduce_max(lmv[:, 4 * s0 + 3:4 * s0 + 4], xfm[p][:], AX.X,
                          apply_absolute_value=True),
             waits=[("xn", E.now("xn")), ("ini", 1)], inc="st")
        E.op("vector", lambda e, s0=s0:
             e.tensor_max(gmax_run[:], gmax_run[:], lmv[:, 4 * s0 + 3:4 * s0 + 4]),
             waits=[("st", E.now("st"))], inc="gm")

    def b_block(m, j, part=2):
        p2 = m % 2
        lxs = "lx%d" % p2
        xcs = "xc%d" % p2
        s0 = 4 * p2
        if part == 1:
            b_state[0] = (m, j)
        waits = [("st", lnst_free.get(p2, 0)), ("xn", xA_free.get("a", 0)),
                 ("xc0", E.now("xc0")), ("xc1", E.now("xc1"))]
        E.op("sync", lambda e, jm=j: e.dma_start(
             out=xA[0][:], in_=xm[512 * jm:512 * (jm + 1), :].rearrange("(k p) d -> p k d", p=128)),
             waits=waits, inc=lxs, k=16)
        lxv = E.now(lxs)
        for tt_ in range(4):
            slot = s0 + tt_
            xap = xA[0][:, D * tt_:D * (tt_ + 1)]
            E.op("vector", lambda e, slot=slot, xap=xap: (
                 e.bn_stats(lnst[:, 12 * slot:12 * slot + 6], xap[:, 0:512]),
                 e.bn_stats(lnst[:, 12 * slot + 6:12 * slot + 12], xap[:, 512:1024]))[-1],
                 waits=[(lxs, lxv)] if tt_ == 0 else [], inc="st")
        for tt_ in range(4):
            slot = s0 + tt_
            E.op("vector", lambda e, slot=slot:
                 e.bn_aggr(lmv[:, 4 * slot:4 * slot + 2], lnst[:, 12 * slot:12 * slot + 12]),
                 waits=[("st", E.now("st"))] if tt_ == 0 else [], inc="st")
        lnst_free[p2] = E.now("st")
        if part == 1:
            return
        b_back(m, j)

    def b_back(m, j):
        p2 = m % 2
        xcs = "xc%d" % p2
        s0 = 4 * p2
        E.op("scalar", lambda e, s0=s0:
             e.activation(out=lmv[:, 4 * s0 + 2:4 * s0 + 15:4],
                          in_=lmv[:, 4 * s0 + 1:4 * s0 + 14:4], func=AF.Sqrt, bias=EPS),
             waits=[("st", E.now("st")), ("ini", 1)], inc="sq")
        E.op("vector", lambda e, s0=s0:
             e.reciprocal(lmv[:, 4 * s0 + 1:4 * s0 + 14:4], lmv[:, 4 * s0 + 2:4 * s0 + 15:4]),
             waits=[("sq", E.now("sq"))], inc="st")
        stv = E.now("st")
        E.op("vector", lambda e, s0=s0, j=j: (
             e.tensor_copy(stats_mr[:, 4 * j:4 * j + 4], lmv[:, 4 * s0:4 * s0 + 16:4]),
             e.tensor_copy(stats_mr[:, 32 + 4 * j:32 + 4 * j + 4],
                           lmv[:, 4 * s0 + 1:4 * s0 + 16:4]))[-1],
             waits=[("st", stv)], inc="st")
        for tt_ in range(4):
            slot = s0 + tt_
            xap = xA[0][:, D * tt_:D * (tt_ + 1)]
            E.op("vector", lambda e, slot=slot, xap=xap:
                 e.tensor_scalar(xap, xap, lmv[:, 4 * slot:4 * slot + 1],
                                 lmv[:, 4 * slot + 1:4 * slot + 2], ALU.subtract, ALU.mult),
                 waits=[], inc="xn")
        xA_free["a"] = E.now("xn")
        E.op("gpsimd", lambda e, jm=j: e.dma_start(
             out=xnm_sc[512 * jm:512 * (jm + 1), :].rearrange("(k p) d -> p k d", p=128),
             in_=xA[0][:]),
             waits=[("xn", E.now("xn"))], inc=xcs, k=16)

    a_js = [j for kind, j in sched if kind == "A"]
    aidx = 0
    bidx = 0
    kv_pend = []
    NA = len(a_js)
    b_state = [None]
    sched_a = [(i, kj) for i, kj in enumerate(sched) if kj[0] == "A"]
    b_after = {}
    for i, (kind, j) in enumerate(sched):
        if kind == "B":
            prev_a = max(ai for ai, (si, _) in enumerate(sched_a) if si < i)
            b_after[prev_a] = j
    for kidx, (si, (kind, j)) in enumerate(sched_a):
        k = aidx
        if k == 0:
            emit_ld(0)
            a_scrm(0)
            a_squares(0)
        if k + 1 < NA:
            emit_ld(k + 1)
        a_sums(k)
        if k + 1 < NA:
            a_scrm(k + 1)
        if k in b_after:
            b_block(bidx, b_after[k], part=1)
        if len(kv_pend) == 2:
            kv_stage(*kv_pend.pop(0))
        if k in b_after:
            b_back(bidx, b_after[k])
            bidx += 1
        if k + 1 < NA:
            a_squares(k + 1)
        a_norm(k, j)
        xfm_built[k % 2] = E.now("xn")
        kv_pend.append((j, k % 2))
        aidx += 1
    while kv_pend:
        kv_stage(*kv_pend.pop(0))

    P1_XN = E.now("xn")
    P1_VSP = E.now("vsp")
    P1_SPILLS = [(n, E.now(n)) for n in ("kspk0", "kspk1", "kspv0", "kspv1")]
    P1_XCS = [(n, E.now(n)) for n in ("xc0", "xc1")]
    P1_GM = E.now("gm")

    E.op("tensor", lambda e: e.transpose(sc_ps[0][0:1, 0:128], gmax_run[:], idn_sb[:]),
         waits=[("gm", E.now("gm")), kq_free[0]], inc="tp")
    E.op("vector", lambda e: e.reduce_max(gstat[0:1, 0:1], sc_ps[0][0:1, 0:128], AX.X),
         waits=[("tp", E.now("tp"))], inc="gm")
    kq_free[0] = ("gm", E.now("gm"))
    E.op("sync", lambda e: e.dma_start(out=ar1_in[:], in_=gstat[:]),
         waits=[("gm", E.now("gm")), ("ini", 1)], inc="arx", k=16)
    E.op("gpsimd", lambda e: e.collective_compute(
        "AllReduce", ALU.max, replica_groups=[[0, 1, 2, 3, 4, 5, 6, 7]],
        ins=[ar1_in[:]], outs=[ar1_out[:]]),
        waits=[("arx", E.now("arx"))], inc="cc1")

    E.op("vector", lambda e: e.tensor_copy(vsum_row[:], pl_ps[0:1, 0:D]),
         waits=[("vsp", E.now("vsp"))], inc="vme")
    pl_free = ("vme", E.now("vme"))
    E.op("tensor", lambda e: [e.transpose(sc_ps[1][:, o:o + 1],
                                          vsum_row[0:1, 128 * o:128 * (o + 1)], idn_sb[0:1, 0:1])
                              for o in range(NC)][-1],
         waits=[("vme", E.now("vme")), kq_free[1]], inc="tp")
    E.op("vector", lambda e: e.tensor_scalar(vmean[:], sc_ps[1][:, 0:NC], 1.0 / float(S), None, ALU.mult),
         waits=[("tp", E.now("tp"))], inc="vme")
    kq_free[1] = ("vme", E.now("vme"))

    E.op("gpsimd", lambda e: e.dma_start(out=wo_sb[:], in_=wo_h.rearrange("(c p) n -> p c n", p=128)),
         waits=[("kpe", E.now("kpe"))] + P1_SPILLS, inc="wo", k=16)

    P1D = [("xn", P1_XN), ("vsp", P1_VSP), ("gm", P1_GM)] + P1_SPILLS + P1_XCS

    wld_free = [("c0", 0), ("c0", 0)]
    wqo_free = [("c0", 0)]
    first_wprep = [True]

    def wprep_chunk(job_idx, src_h, dst_sc, row0, ncols, wsts):
        sl = job_idx % 2
        wfs = "wfl%d" % sl
        w0 = [wld_free[sl]] + (P1D if first_wprep[0] else [])
        E.op("sync", lambda e, sl=sl, src_h=src_h, row0=row0, ncols=ncols:
             e.dma_start(out=wld[sl][:, 0:ncols], in_=src_h[row0:row0 + 128, 0:ncols]),
             waits=w0, inc=wfs, k=16)
        first_wprep[0] = False
        E.op("vector", lambda e, sl=sl, ncols=ncols:
             e.reduce_max(wqs[:, 0:1], wld[sl][:, 0:ncols], AX.X, apply_absolute_value=True),
             waits=[(wfs, E.now(wfs))], inc="wqd")
        E.op("vector", lambda e:
             e.tensor_scalar(wqs[:, 1:2], wqs[:, 0:1], inv127, None, ALU.mult),
             waits=[("wqd", E.now("wqd"))], inc="wqd")
        E.op("vector", lambda e: e.reciprocal(wqs[:, 2:3], wqs[:, 1:2]),
             waits=[("wqd", E.now("wqd"))], inc="wqd")
        E.op("vector", lambda e, sl=sl, ncols=ncols:
             e.tensor_scalar(wqt[:, 0:ncols], wld[sl][:, 0:ncols], wqs[:, 2:3], None, ALU.mult),
             waits=[("wqd", E.now("wqd"))], inc="wqd")
        wld_free[sl] = ("wqd", E.now("wqd"))
        E.op("scalar", lambda e, ncols=ncols:
             e.activation(out=wqi[:, 0:ncols], in_=wqt[:, 0:ncols], func=AF.Identity),
             waits=[("wqd", E.now("wqd"))], inc="wqd")
        E.op("scalar", lambda e, ncols=ncols:
             e.activation(out=wqt[:, 0:ncols], in_=wqi[:, 0:ncols], func=AF.Identity),
             waits=[("wqd", E.now("wqd"))], inc="wqd")
        E.op("vector", lambda e, ncols=ncols:
             e.tensor_scalar(wqo[:, 0:ncols], wqt[:, 0:ncols], wqs[:, 1:2], None, ALU.mult),
             waits=[("wqd", E.now("wqd")), wqo_free[0]], inc="wqd")
        E.op("gpsimd", lambda e, dst_sc=dst_sc, row0=row0, ncols=ncols:
             e.dma_start(out=dst_sc[row0:row0 + 128, 0:ncols], in_=wqo[:, 0:ncols]),
             waits=[("wqd", E.now("wqd"))], inc=wsts, k=16)
        wqo_free[0] = (wsts, E.now(wsts))

    wprep_jobs = ([(wg_h, wgq_sc, 128 * r, D, "wstg") for r in range(NI)]
                  + [(wu_h, wuq_sc, 128 * r, D, "wstu") for r in range(NI)]
                  + [(wd_h, wdq_sc, 128 * r, DI, "wstd") for r in range(NC)])
    job_idx = [0]

    per_u_box = [0]

    def emit_jobs(count):
        while count > 0 and job_idx[0] < len(wprep_jobs):
            src_h, dst_sc, row0, ncols, wsts = wprep_jobs[job_idx[0]]
            wprep_chunk(job_idx[0], src_h, dst_sc, row0, ncols, wsts)
            job_idx[0] += 1
            count -= 1

    qxn_free = [("c0", 0), ("c0", 0)]
    kst_free = [("c0", 0), ("c0", 0)]
    vst_free = [("c0", 0), ("c0", 0)]
    exp_free = [("c0", 0)] * 4
    ao_free = [("c0", 0), ("c0", 0)]
    q_free = [("c0", 0), ("c0", 0)]
    onorm_free = [("c0", 0)]
    pl_free_box = [pl_free]
    first_p2 = {"qxt": True, "qev": True, "exp": True, "oev": True, "mk": True}

    qev_of = {}

    def qproj(u):
        up = u % 2
        qxs = "qx%d" % up
        w0 = P1_XCS + [qxn_free[up]] + (P1D if first_p2["qxt"] else [])
        first_p2["qxt"] = False
        E.op("sync", lambda e, sh, u=u, up=up: [
             e.dma_start(out=qxn[up][:, 256 * c:256 * (c + 1)],
                         in_=xnm_sc[256 * u:256 * (u + 1), 128 * c:128 * (c + 1)],
                         transpose=True).then_inc(sh, 16) for c in range(NC)][-1],
             waits=w0, inc=qxs, k=16 * NC, self_inc=True)
        qxv = E.now(qxs)
        for o in range(NC):
            bank = o % 2
            E.op("tensor", lambda e, up=up, o=o, bank=bank: [
                 e.matmul(sc_ps[bank][:, 0:256],
                          wq_sb[:, c * D + o * 128:c * D + o * 128 + 128],
                          qxn[up][:, 256 * c:256 * (c + 1)],
                          start=(c == 0), stop=(c == NC - 1)) for c in range(NC)][-1],
                 waits=[(qxs, qxv), kq_free[bank]], inc="qpe")
            wv_ = [("qpe", E.now("qpe"))]
            if o == 0:
                wv_.append(q_free[up])
                if first_p2["qev"]:
                    wv_ += P1D
                    first_p2["qev"] = False
            E.op("scalar", lambda e, up=up, o=o, bank=bank:
                 e.activation(out=q_sb[up][:, 256 * o:256 * (o + 1)], in_=sc_ps[bank][:, 0:256],
                              func=AF.Identity, bias=bq_sb[:, o:o + 1]),
                 waits=wv_, inc="qev")
            kq_free[bank] = ("qev", E.now("qev"))
        qxn_free[up] = ("qpe", E.now("qpe"))
        qev_of[u] = E.now("qev")

    def p2_supertile(u):
        up = u % 2
        n_i = NB - 4 * u
        qev_v = qev_of[u]

        jobs_at = max(2, (3 * (n_i + 2)) // 4)
        pv_pend = {}
        for step in range(n_i + 2):
            if step == jobs_at:
                emit_jobs(per_u_box[0])
            if step < n_i:
                i = step
                il = i % 4
                gg = u + i // 4
                buf = gg % 2
                klds = "kld%d" % buf
                vlds = "vld%d" % buf
                if il == 0:
                    E.op("sync", lambda e, gg=gg, buf=buf:
                         e.dma_start(out=kst[buf][:],
                                     in_=K_sc.rearrange("o p s -> p o s")[:, :, 512 * gg:512 * (gg + 1)]),
                         waits=P1_SPILLS + [kst_free[buf]], inc=klds, k=16)
                    E.op("gpsimd", lambda e, gg=gg, buf=buf:
                         e.dma_start(out=vst[buf][:],
                                     in_=V_sc[512 * gg:512 * (gg + 1), :].rearrange("(k p) d -> p k d", p=128)),
                         waits=P1_SPILLS + [vst_free[buf]], inc=vlds, k=16)
                kldv, vldv = E.now(klds), E.now(vlds)
                sbank = i % 2
                slot = i % 4

                E.op("tensor", lambda e, up=up, i=i, il=il, buf=buf, sbank=sbank: [
                     e.matmul(sc_ps[sbank][:, 0:256],
                              kst[buf][:, 512 * c + 128 * il:512 * c + 128 * il + 128],
                              q_sb[up][:, 256 * c:256 * (c + 1)],
                              start=(c == 0), stop=(c == NC - 1)) for c in range(NC)][-1],
                     waits=[(klds, kldv), ("qev", qev_v), kq_free[sbank]], inc="sc")
                if il == 3 or i == n_i - 1:
                    kst_free[buf] = ("sc", E.now("sc"))

                w0 = [("sc", E.now("sc")), exp_free[slot]] + (P1D if first_p2["exp"] else [])
                first_p2["exp"] = False
                E.op("scalar", lambda e, slot=slot, sbank=sbank:
                     e.activation(out=exps[:, 256 * slot:256 * (slot + 1)],
                                  in_=sc_ps[sbank][:, 0:256], func=AF.Exp, scale=exp_scale),
                     waits=w0, inc="ex")
                kq_free[sbank] = ("ex", E.now("ex"))

                if i <= 4:
                    w0 = [("ex", E.now("ex")), ("c0", C0)] + (P1D if first_p2["mk"] else [])
                    first_p2["mk"] = False
                    E.op("vector", lambda e, i=i, slot=slot:
                         e.tensor_mul(exps[:, 256 * slot:256 * (slot + 1)],
                                      exps[:, 256 * slot:256 * (slot + 1)],
                                      msk_sb[:, 256 * i:256 * (i + 1)]),
                         waits=w0, inc="mk")
                    pv_pend[i] = (il, buf, slot, ("mk", E.now("mk")), vlds, vldv)
                else:
                    pv_pend[i] = (il, buf, slot, ("ex", E.now("ex")), vlds, vldv)

            if step >= 2:
                i = step - 2
                il, buf, slot, pv_wait, vlds, vldv = pv_pend.pop(i)
                wv_ = [pv_wait, (vlds, vldv), ("ini", 1)]
                if i == 0:
                    wv_ += [po_free[0], po_free[1], po_free[2], po_free[3], pl_free_box[0]]

                def pvf(e, i=i, il=il, buf=buf, slot=slot, n_i=n_i):
                    for m in range(NC):
                        e.matmul(po_ps[m // 2][:, 256 * (m % 2):256 * (m % 2) + 256],
                                 vst[buf][:, 1024 * il + 128 * m:1024 * il + 128 * m + 128],
                                 exps[:, 256 * slot:256 * (slot + 1)],
                                 start=(i == 0 and (m % 2 == 0)), stop=(i == n_i - 1),
                                 skip_group_check=True)
                    return e.matmul(pl_ps[0:1, 0:256], ones_c[:], exps[:, 256 * slot:256 * (slot + 1)],
                                    start=(i == 0), stop=(i == n_i - 1), skip_group_check=True)
                E.op("tensor", pvf, waits=wv_, inc="pv")
                exp_free[slot] = ("pv", E.now("pv"))
                if il == 3 or i == n_i - 1:
                    vst_free[buf] = ("pv", E.now("pv"))

        if u == NU - 1:
            E.op("vector", lambda e: e.tensor_add(l_raw[:], pl_ps[0:1, 0:256], fxl_sb[:]),
                 waits=[("pv", E.now("pv")), ("c0", C0)], inc="epi")
        else:
            E.op("vector", lambda e: e.tensor_copy(l_raw[:], pl_ps[0:1, 0:256]),
                 waits=[("pv", E.now("pv"))], inc="epi")
        E.op("vector", lambda e: e.reciprocal(linv_sb[:], l_raw[:]),
             waits=[("epi", E.now("epi"))], inc="epi")
        pl_free_box[0] = ("epi", E.now("epi"))
        E.op("tensor", lambda e: e.matmul(sc_ps[0][:, 0:256], ones_r[:], linv_sb[:],
                                          start=True, stop=True),
             waits=[("epi", E.now("epi")), kq_free[0]], inc="bc2")
        E.op("vector", lambda e: e.tensor_copy(linv_bc[:], sc_ps[0][:, 0:256]),
             waits=[("bc2", E.now("bc2"))], inc="epi")
        kq_free[0] = ("epi", E.now("epi"))

        if u + 1 < NU:
            qproj(u + 1)

        for m in range(NC):
            wv_ = [("epi", E.now("epi"))]
            if m == 0:
                wv_.append(onorm_free[0])
            E.op("vector", lambda e, m=m:
                 e.tensor_mul(onorm[:, 256 * m:256 * (m + 1)],
                              po_ps[m // 2][:, 256 * (m % 2):256 * (m % 2) + 256], linv_bc[:]),
                 waits=wv_, inc="epi")
        for b_ in range(4):
            po_free[b_] = ("epi", E.now("epi"))

        if u == NU - 1:
            for m in range(NC):
                E.op("vector", lambda e, m=m:
                     e.tensor_mul(onorm[:, 256 * m:256 * (m + 1)],
                                  onorm[:, 256 * m:256 * (m + 1)], fix_sb[:, 0:256]),
                     waits=[("epi", E.now("epi")), ("vme", E.now("vme"))], inc="epi")
                E.op("vector", lambda e, m=m:
                     e.scalar_tensor_tensor(onorm[:, 256 * m:256 * (m + 1)],
                                            fix_sb[:, 256:512], vmean[:, m:m + 1],
                                            onorm[:, 256 * m:256 * (m + 1)], ALU.mult, ALU.add),
                     waits=[("epi", E.now("epi"))], inc="epi")

        for o in range(NC):
            bank = o % 2
            E.op("tensor", lambda e, o=o, bank=bank: [
                 e.matmul(sc_ps[bank][:, 0:256],
                          wo_sb[:, m * D + o * 128:m * D + o * 128 + 128],
                          onorm[:, 256 * m:256 * (m + 1)],
                          start=(m == 0), stop=(m == NC - 1)) for m in range(NC)][-1],
                 waits=[("epi", E.now("epi")), ("wo", 16), kq_free[bank]], inc="ope")
            wv_ = [("ope", E.now("ope"))]
            if o == 0:
                wv_.append(ao_free[up])
                if first_p2["oev"]:
                    wv_ += P1D
                    first_p2["oev"] = False
            E.op("scalar", lambda e, up=up, o=o, bank=bank:
                 e.activation(out=ao_st[up][:, 256 * o:256 * (o + 1)], in_=sc_ps[bank][:, 0:256],
                              func=AF.Identity, bias=bo_sb[:, o:o + 1]),
                 waits=wv_, inc="ob")
            kq_free[bank] = ("ob", E.now("ob"))
        onorm_free[0] = ("ope", E.now("ope"))

        aoss = "aos%d" % up
        E.op("sync", lambda e, u=u, up=up:
             e.dma_start(out=ao_sc.rearrange("(o p) q -> p o q", p=128)[:, :, 256 * u:256 * (u + 1)],
                         in_=ao_st[up][:]),
             waits=[("ob", E.now("ob"))], inc=aoss, k=16)
        ao_free[up] = (aoss, E.now(aoss))
        q_free[up] = ("sc", E.now("sc"))

    per_u_box[0] = int(math.ceil(len(wprep_jobs) / max(1, NU - 2)))
    qproj(0)
    for u in range(NU):
        p2_supertile(u)
    emit_jobs(len(wprep_jobs))
    P2_OPE = E.now("ope")
    P2_OB = E.now("ob")
    P2_AOS = [("aos0", E.now("aos0")), ("aos1", E.now("aos1"))]
    P2_PV = E.now("pv")
    P2_SC = E.now("sc")
    P2_QPE = E.now("qpe")

    def scale_chain(ar_out_ap, clip_sb, s_bc, r_bc, cc_sem, cc_val):
        E.op("sync", lambda e: e.dma_start(out=gstat[0:1, 8:16], in_=ar_out_ap[0:1, 0:8]),
             waits=[(cc_sem, cc_val)], inc="arx", k=16)
        arv = E.now("arx")
        E.op("scalar", lambda e: e.activation(out=scl[0:1, 0:1], in_=clip_sb[:], func=AF.Sigmoid),
             waits=[("c0", C0), ("sch", E.now("sch"))], inc="sch")
        E.op("vector", lambda e: e.tensor_mul(scl[0:1, 1:2], scl[0:1, 0:1], gstat[0:1, 8:9]),
             waits=[("sch", E.now("sch")), ("arx", arv)], inc="sch")
        E.op("vector", lambda e: e.tensor_scalar(scl[0:1, 2:3], scl[0:1, 1:2], inv127, None, ALU.mult),
             waits=[("sch", E.now("sch"))], inc="sch")
        E.op("vector", lambda e: e.reciprocal(scl[0:1, 3:4], scl[0:1, 2:3]),
             waits=[("sch", E.now("sch"))], inc="sch")
        E.op("tensor", lambda e: (
            e.matmul(sc_ps[0][:, 256:257], ones_r[:], scl[0:1, 2:3], start=True, stop=True),
            e.matmul(sc_ps[0][:, 257:258], ones_r[:], scl[0:1, 3:4], start=True, stop=True))[-1],
            waits=[("sch", E.now("sch")), kq_free[0]], inc="tp2")
        E.op("vector", lambda e, s_bc=s_bc, r_bc=r_bc: (
            e.tensor_copy(s_bc[:], sc_ps[0][:, 256:257]),
            e.tensor_copy(r_bc[:], sc_ps[0][:, 257:258]))[-1],
            waits=[("tp2", E.now("tp2"))], inc="sch")
        kq_free[0] = ("sch", E.now("sch"))
        return E.now("sch")

    sx_ready = scale_chain(ar1_out, cx_sb, sx_bc, rx_bc, "cc1", E.now("cc1"))

    P2D = ([("sc", P2_SC), ("pv", P2_PV), ("qpe", P2_QPE), ("ope", P2_OPE),
            ("ob", P2_OB), ("wqd", E.now("wqd")), ("ex", E.now("ex")),
            ("epi", E.now("epi")), ("mk", E.now("mk")), ("bc2", E.now("bc2")),
            ("wstg", 16 * NI), ("wstu", 16 * NI), ("wstd", 16 * NC)]
           + P2_AOS + [("wfl0", E.now("wfl0")), ("wfl1", E.now("wfl1"))])
    E.op("sync", lambda e, sh: [e.dma_start(out=wgT[:, DI * hc:DI * (hc + 1)],
                                        in_=wgq_sc[:, 128 * hc:128 * (hc + 1)], transpose=True).then_inc(sh, 16)
                            for hc in range(NC)][-1],
         waits=P2D + P1D + [("gm", E.now("gm"))], inc="wt", k=16 * NC, self_inc=True)
    E.op("sync", lambda e, sh: [e.dma_start(out=wuT[:, DI * hc:DI * (hc + 1)],
                                        in_=wuq_sc[:, 128 * hc:128 * (hc + 1)], transpose=True).then_inc(sh, 16)
                            for hc in range(NC)][-1],
         waits=[("wstu", 16 * NI)], inc="wt", k=16 * NC, self_inc=True)
    wt_gu = E.now("wt")

    xq_free = [("c0", 0), ("c0", 0)]
    xqt_free = [("c0", 0)] * 4
    t_free = [("c0", 0)] * 4
    g_free = [("c0", 0), ("c0", 0)]
    first_fa = [True]

    xqd_of = {}
    xq_ctx = {}

    def xq_load(f):
        fp = f % 2
        w0 = list(P1_XCS) + [xq_free[fp]]
        if first_fa[0]:
            w0 += [("ope", P2_OPE), ("ob", P2_OB), ("qpe", P2_QPE)]
            first_fa[0] = False
        E.op("sync", lambda e, sh, f=f, fp=fp: [
             e.dma_start(out=xq_sb[fp][:, 512 * c:512 * (c + 1)],
                         in_=xnm_sc[512 * f:512 * (f + 1), 128 * c:128 * (c + 1)],
                         transpose=True).then_inc(sh, 16) for c in range(NC)][-1],
             waits=w0, inc="xqt%d" % fp, k=16 * NC, self_inc=True)
        xq_ctx[f] = ("xqt%d" % fp, E.now("xqt%d" % fp), {})

    def q_stage1(f, c):
        fp = f % 2
        xqts, xqtv, st2q = xq_ctx[f]
        cs = c % 4
        E.op("scalar", lambda e, fp=fp, c=c, cs=cs:
             e.activation(out=xqi[cs][:], in_=xq_sb[fp][:, 512 * c:512 * (c + 1)],
                          func=AF.Identity, scale=rx_bc[:, 0:1]),
             waits=[(xqts, xqtv), ("sch", sx_ready), xqt_free[cs]], inc="xqa")
        E.op("vector", lambda e, cs=cs:
             e.tensor_scalar(xqi[cs][:], xqi[cs][:], -QMAX - 1.0, QMAX, ALU.max, ALU.min),
             waits=[("xqa", E.now("xqa"))], inc="xqd")
        st2q[c] = E.now("xqd")

    def q_stage2(f, c):
        fp = f % 2
        st2q = xq_ctx[f][2]
        cs = c % 4
        E.op("scalar", lambda e, fp=fp, c=c, cs=cs:
             e.activation(out=xq_sb[fp][:, 512 * c:512 * (c + 1)], in_=xqi[cs][:],
                          func=AF.Identity, scale=sx_bc[:, 0:1]),
             waits=[("xqd", st2q[c])], inc="xqa")
        xqt_free[cs] = ("xqa", E.now("xqa"))

    def xq_wave(f, c):
        q_stage1(f, c)
        if c >= 3:
            q_stage2(f, c - 3)
        if c == NC - 1:
            for cc in range(NC - 3, NC):
                q_stage2(f, cc)
            xqd_of[f] = E.now("xqa")

    xq_load(0)
    for c in range(NC):
        xq_wave(0, c)
    for f in range(NF):
        fp = f % 2
        xqdv = xqd_of[f]

        for ic in range(NI):
            if f + 1 < NF:
                if ic == 4:
                    xq_load(f + 1)
                elif ic >= 6 and ic <= 20 and ic % 2 == 0:
                    xq_wave(f + 1, (ic - 6) // 2)
            gb = ic % 2
            E.op("tensor", lambda e, fp=fp, ic=ic, gb=gb: [
                 e.matmul(sc_ps[gb][:, 0:512],
                          wgT[:, DI * hc + 128 * ic:DI * hc + 128 * ic + 128],
                          xq_sb[fp][:, 512 * hc:512 * (hc + 1)],
                          start=(hc == 0), stop=(hc == NC - 1)) for hc in range(NC)][-1],
                 waits=[("xqa", xqdv), ("wt", wt_gu), kq_free[gb]], inc="gu")
            gu_g = E.now("gu")
            E.op("tensor", lambda e, fp=fp, ic=ic, gb=gb: [
                 e.matmul(po_ps[gb][:, 0:512],
                          wuT[:, DI * hc + 128 * ic:DI * hc + 128 * ic + 128],
                          xq_sb[fp][:, 512 * hc:512 * (hc + 1)],
                          start=(hc == 0), stop=(hc == NC - 1)) for hc in range(NC)][-1],
                 waits=[po_free[gb]], inc="gu")
            E.op("scalar", lambda e, gb=gb:
                 e.activation(out=g_sb[gb][:], in_=sc_ps[gb][:, 0:512], func=AF.Silu),
                 waits=[("gu", gu_g), g_free[gb]], inc="sil")
            kq_free[gb] = ("sil", E.now("sil"))

            slot = (f * NI + ic) % 4
            E.op("vector", lambda e, gb=gb, slot=slot:
                 e.tensor_tensor(t_sl[slot][:], po_ps[gb][:, 0:512], g_sb[gb][:], ALU.mult),
                 waits=[("sil", E.now("sil")), ("gu", E.now("gu")), t_free[slot]], inc="tt")
            po_free[gb] = ("tt", E.now("tt"))
            g_free[gb] = ("tt", E.now("tt"))
            E.op("vector", lambda e, slot=slot:
                 e.reduce_max(tstat[:, 0:1], t_sl[slot][:], AX.X, apply_absolute_value=True),
                 waits=[("tt", E.now("tt")), ("ini", 1)], inc="tt")
            E.op("vector", lambda e: e.tensor_max(tmax_run[:], tmax_run[:], tstat[:, 0:1]),
                 waits=[("tt", E.now("tt"))], inc="tt")
            tsts = "tst%d" % slot
            E.op("gpsimd", lambda e, f=f, ic=ic, slot=slot:
                 e.dma_start(out=t_sc[ic, :, 512 * f:512 * (f + 1)], in_=t_sl[slot][:]),
                 waits=[("tt", E.now("tt"))], inc=tsts, k=16)
            t_free[slot] = (tsts, E.now(tsts))
        xq_free[fp] = ("gu", E.now("gu"))

    E.op("tensor", lambda e: e.transpose(sc_ps[0][0:1, 0:128], tmax_run[:], idn_sb[:]),
         waits=[("tt", E.now("tt")), kq_free[0]], inc="tp2")
    E.op("vector", lambda e: e.reduce_max(gstat[0:1, 0:1], sc_ps[0][0:1, 0:128], AX.X),
         waits=[("tp2", E.now("tp2")), ("arx", 32)], inc="sch")
    kq_free[0] = ("sch", E.now("sch"))
    E.op("sync", lambda e: e.dma_start(out=ar2_in[:], in_=gstat[:]),
         waits=[("sch", E.now("sch"))], inc="arx", k=16)
    E.op("gpsimd", lambda e: e.collective_compute(
        "AllReduce", ALU.max, replica_groups=[[0, 1, 2, 3, 4, 5, 6, 7]],
        ins=[ar2_in[:]], outs=[ar2_out[:]]),
        waits=[("arx", E.now("arx"))], inc="cc2")

    E.op("sync", lambda e, sh: [e.dma_start(out=wdT[:, 1024 * ic:1024 * (ic + 1)],
                                        in_=wdq_sc[:, 128 * ic:128 * (ic + 1)], transpose=True).then_inc(sh, 16)
                            for ic in range(NI)][-1],
         waits=[("wstd", 16 * NC), ("gu", E.now("gu")), ("tt", E.now("tt")),
                ("xqd", E.now("xqd")), ("xqa", E.now("xqa")), ("sil", E.now("sil"))]
               + [("tst%d" % i_, E.now("tst%d" % i_)) for i_ in range(4)],
         inc="wt", k=16 * NI, self_inc=True)
    wt_d = E.now("wt")

    sh_ready = scale_chain(ar2_out, cd_sb, sh_bc, rh_bc, "cc2", E.now("cc2"))

    GU_ALL = E.now("gu")
    TT_ALL = E.now("tt")
    TST_ALL = [("tst%d" % i_, E.now("tst%d" % i_)) for i_ in range(4)]
    SIL_ALL = E.now("sil")
    tB_free = [("gu", GU_ALL), ("gu", GU_ALL)]
    hB_free = [("gu", GU_ALL), ("gu", GU_ALL)]
    tBi_first = [True]
    xB_free = [("gu", GU_ALL), ("gu", GU_ALL)]
    aoB_free = [TST_ALL[0], TST_ALL[1], TST_ALL[2], TST_ALL[3]]
    outB_free = [TST_ALL[2], TST_ALL[3]]
    dn_banks = [kq_free[0], kq_free[1], po_free[0], po_free[1]]

    fb_pend = {}
    for step in range(NF2 + 1):
        if step < NF2:
            f2 = step
            fp = f2 % 2
            E.op("sync", lambda e, f2=f2, fp=fp:
                 e.dma_start(out=tB[fp][:],
                             in_=t_sc.rearrange("i p s -> p i s")[:, :, 256 * f2:256 * (f2 + 1)]),
                 waits=TST_ALL + [tB_free[fp]], inc="tld%d" % fp, k=16)
            w0 = [xB_free[fp]]
            if f2 < 2:
                w0 += [("ope", P2_OPE), ("qpe", P2_QPE)]
            E.op("sync", lambda e, f2=f2, fp=fp:
                 e.dma_start(out=xB[fp][:], in_=xm[256 * f2:256 * (f2 + 1), :]
                             .rearrange("(k p) d -> p k d", p=128)),
                 waits=w0, inc="xbl%d" % fp, k=16)
            xblv = ("xbl%d" % fp, E.now("xbl%d" % fp))
            aotv = []
            for g in range(2):
                sl4 = 2 * fp + g
                E.op("sync", lambda e, f2=f2, g=g, sl4=sl4:
                     e.dma_start(out=aoB[sl4][:], in_=ao_sc[:, 128 * (2 * f2 + g):128 * (2 * f2 + g + 1)],
                                 transpose=True),
                     waits=P2_AOS + [aoB_free[sl4]], inc="aot%d" % sl4, k=16)
                aotv.append(("aot%d" % sl4, E.now("aot%d" % sl4)))

            QW = NI * 256 // 4
            tldw = ("tld%d" % fp, E.now("tld%d" % fp))
            for qw in range(4):
                sl = slice(QW * qw, QW * (qw + 1))
                w0 = [tldw, ("sch", sh_ready)] if qw == 0 else [tldw]
                if tBi_first[0]:
                    w0 += [("sil", SIL_ALL), ("gu", GU_ALL)]
                    tBi_first[0] = False
                E.op("scalar", lambda e, fp=fp, sl=sl:
                     e.activation(out=tBi[:, sl], in_=tB[fp][:, sl], func=AF.Identity,
                                  scale=rh_bc[:, 0:1]),
                     waits=w0, inc="hqa")
                E.op("vector", lambda e, sl=sl:
                     e.tensor_scalar(tBi[:, sl], tBi[:, sl], -QMAX - 1.0, QMAX, ALU.max, ALU.min),
                     waits=[("hqa", E.now("hqa"))], inc="hq")
                w1 = [("hq", E.now("hq"))]
                if qw == 0:
                    w1.append(hB_free[fp])
                E.op("scalar", lambda e, fp=fp, sl=sl:
                     e.activation(out=hB[fp][:, sl], in_=tBi[:, sl], func=AF.Identity,
                                  scale=sh_bc[:, 0:1]),
                     waits=w1, inc="hqa")
            hqv = E.now("hqa")
            tB_free[fp] = ("hqa", hqv)
            fb_pend[f2] = (fp, hqv, xblv, aotv)

        if step >= 1:
            f2 = step - 1
            fp, hqv, xblv, aotv = fb_pend.pop(f2)
            for g in range(2):
                for hb in range(2):
                    bidx = (2 * g + hb)
                    bank = [sc_ps[0], sc_ps[1], po_ps[0], po_ps[1]][bidx]
                    E.op("tensor", lambda e, fp=fp, g=g, hb=hb, bank=bank: [
                         e.matmul(bank[:, 0:512],
                                  hB[fp][:, 256 * ic + 128 * g:256 * ic + 128 * g + 128],
                                  wdT[:, 1024 * ic + 512 * hb:1024 * ic + 512 * hb + 512],
                                  start=(ic == 0), stop=(ic == NI - 1)) for ic in range(NI)][-1],
                         waits=[("hqa", hqv), ("wt", wt_d), dn_banks[bidx]], inc="dn")
                    wv_ = [("dn", E.now("dn")), xblv, aotv[g]]
                    if hb == 0:
                        wv_.append(outB_free[g])
                    E.op("vector", lambda e, fp=fp, g=g, hb=hb, bank=bank:
                         e.tensor_add(outB[g][:, 512 * hb:512 * (hb + 1)], bank[:, 0:512],
                                      xB[fp][:, 1024 * g + 512 * hb:1024 * g + 512 * hb + 512]),
                         waits=wv_, inc="out")
                    E.op("vector", lambda e, fp=fp, g=g, hb=hb:
                         e.tensor_add(outB[g][:, 512 * hb:512 * (hb + 1)],
                                      outB[g][:, 512 * hb:512 * (hb + 1)],
                                      aoB[2 * fp + g][:, 512 * hb:512 * (hb + 1)]),
                         waits=[("out", E.now("out"))], inc="out")
                    dn_banks[bidx] = ("out", E.now("out"))
                E.op("sync", lambda e, f2=f2, g=g:
                     e.dma_start(out=y_h[128 * (2 * f2 + g):128 * (2 * f2 + g + 1), :], in_=outB[g][:]),
                     waits=[("out", E.now("out"))], inc="yst%d" % g, k=16)
                outB_free[g] = ("yst%d" % g, E.now("yst%d" % g))
                aoB_free[2 * fp + g] = ("out", E.now("out"))
            hB_free[fp] = ("dn", E.now("dn"))
            xB_free[fp] = ("out", E.now("out"))

    E.op("sync", lambda e: None,
         waits=[("yst0", E.now("yst0")), ("yst1", E.now("yst1"))])

    with nc.Block() as block:
        @block.sync
        def _(e):
            E.replay("sync", e)

        @block.gpsimd
        def _(e):
            E.replay("gpsimd", e)

        @block.scalar
        def _(e):
            E.replay("scalar", e)

        @block.vector
        def _(e):
            E.replay("vector", e)

        @block.tensor
        def _(e):
            E.replay("tensor", e)

    return nc


def _make_masks(h, S):
    m = np.ones((5, 128, 256), dtype=np.float32)
    s_loc = np.arange(128)[:, None]
    for r in range(5):
        for half, qoff in ((0, h), (1, 2 + h)):
            q_loc = np.arange(128)[None, :]
            masked = (r - qoff) * 128 + s_loc <= q_loc + 1
            m[r][:, 128 * half:128 * (half + 1)][masked] = 0.0
    return m.reshape(5 * 128, 256).astype(ml_dtypes.bfloat16)


def _make_fixes(h, S):
    fm = np.ones((128, 256), dtype=np.float32)
    fa = np.zeros((128, 256), dtype=np.float32)
    fl = np.zeros((1, 256), dtype=np.float32)
    if h == 1:
        fm[:, 254:256] = 0.0
        fa[:, 254:256] = 1.0
        fl[0, 254:256] = 1.0
    fixes = np.concatenate([fm, fa], axis=1).astype(ml_dtypes.bfloat16)
    return fixes, fl


_NC_CACHE = {}


def _get_nc(S, D, DI):
    key = (S, D, DI)
    if key not in _NC_CACHE:
        _NC_CACHE[key] = build_nc(S, D, DI)
    return _NC_CACHE[key]


def kernel(L0_attn_x, L0_attn_wq, L0_attn_bq, L0_attn_wk, L0_attn_bk,
           L0_attn_wv, L0_attn_bv, L0_attn_wo, L0_attn_bo,
           L0_ffn_clip_x, L0_ffn_clip_down, L0_ffn_w_gate, L0_ffn_w_up,
           L0_ffn_w_down, _S=None):
    x = np.ascontiguousarray(np.asarray(L0_attn_x, dtype=np.float32))
    B, S, D = x.shape
    DI = int(np.asarray(L0_ffn_w_gate).shape[0])
    nc = _get_nc(S, D, DI)

    f32 = lambda a: np.ascontiguousarray(np.asarray(a, dtype=np.float32))
    wq, wk, wv, wo = f32(L0_attn_wq), f32(L0_attn_wk), f32(L0_attn_wv), f32(L0_attn_wo)
    bq, bk, bv, bo = (f32(L0_attn_bq).reshape(1, D), f32(L0_attn_bk).reshape(1, D),
                      f32(L0_attn_bv).reshape(1, D), f32(L0_attn_bo).reshape(1, D))
    wg, wu, wd = f32(L0_ffn_w_gate), f32(L0_ffn_w_up), f32(L0_ffn_w_down)
    cx = f32(L0_ffn_clip_x).reshape(1, 1)
    cd = f32(L0_ffn_clip_down).reshape(1, 1)
    ident = np.eye(128, dtype=np.float32)

    in_maps = []
    for core in range(8):
        b, h = core // 2, core % 2
        xb = x[b]
        x_mine = np.ascontiguousarray(xb.reshape(S // 128, 128, D)[h::2].reshape(S // 2, D))
        fixes, fl = _make_fixes(h, S)
        in_maps.append({
            "x_true": xb, "x_true_fm": np.ascontiguousarray(xb.T), "x_mine": x_mine,
            "wq": wq, "wk": wk, "wv": wv, "wo": wo,
            "bq": bq, "bk": bk, "bv": bv, "bo": bo,
            "clip_x": cx, "clip_down": cd,
            "w_gate": wg, "w_up": wu, "w_down": wd,
            "masks": _make_masks(h, S), "fixes": fixes, "fix_l": fl,
            "ident": ident,
        })

    res = run_bass_kernel_spmd(nc, in_maps, core_ids=list(range(8)))
    y = np.empty((B, S, D), dtype=np.float32)
    for core in range(8):
        b, h = core // 2, core % 2
        yv = y[b].reshape(S // 128, 128, D)
        yv[h::2] = np.asarray(res.results[core]["y"]).reshape(S // 256, 128, D)
    return y


# version 20 (final)
# speedup vs baseline: 1.0514x; 1.0514x over previous
import math
import numpy as np
import ml_dtypes
import concourse.bass as bass
import concourse.mybir as mybir
from concourse.bass_utils import run_bass_kernel_spmd

F32 = mybir.dt.float32
BF16 = mybir.dt.bfloat16
I32 = mybir.dt.int32
AF = mybir.ActivationFunctionType
ALU = mybir.AluOpType
AX = mybir.AxisListType

QMAX = 127.0
EPS = 1e-5
ENGS = ("sync", "gpsimd", "scalar", "vector", "tensor")


class Emitter:

    def __init__(self, nc, sem_names):
        self.nc = nc
        self.streams = {k: [] for k in ENGS}
        self.h = {n: nc.alloc_semaphore("s_" + n) for n in sem_names}
        self.n = {n: 0 for n in sem_names}

    def now(self, sem):
        return self.n[sem]

    def op(self, eng, fn, waits=(), inc=None, k=1, self_inc=False):
        rw = []
        for nm, v in waits:
            assert v <= self.n[nm], ("wait ahead of emission", nm, v, self.n[nm])
            if v > 0:
                rw.append((self.h[nm], v))
        ih = None
        if inc is not None:
            self.n[inc] += k
            ih = self.h[inc]
        self.streams[eng].append((rw, fn, ih, k, self_inc))
        return self.n[inc] if inc is not None else None

    def replay(self, eng_name, e):
        for rw, fn, ih, k, self_inc in self.streams[eng_name]:
            for hh, v in rw:
                e.wait_ge(hh, v)
            if self_inc:
                fn(e, ih)
            else:
                ins = fn(e)
                if ih is not None:
                    assert ins is not None
                    ins.then_inc(ih, k)


def build_nc(S=8192, D=1024, DI=2816):
    NB = S // 128
    NU = S // 512
    NT = S // 512
    NFB = S // 1024
    NF = S // 1024
    NF2 = S // 512
    NC = D // 128
    NI = DI // 128
    SM = S // 2

    nc = bass.Bass("TRN2", target_bir_lowering=False)

    xt = nc.dram_tensor("x_true", [S, D], F32, kind="ExternalInput").ap()
    xtf = nc.dram_tensor("x_true_fm", [D, S], F32, kind="ExternalInput").ap()
    xm = nc.dram_tensor("x_mine", [SM, D], F32, kind="ExternalInput").ap()
    wq_h = nc.dram_tensor("wq", [D, D], F32, kind="ExternalInput").ap()
    wk_h = nc.dram_tensor("wk", [D, D], F32, kind="ExternalInput").ap()
    wv_h = nc.dram_tensor("wv", [D, D], F32, kind="ExternalInput").ap()
    wo_h = nc.dram_tensor("wo", [D, D], F32, kind="ExternalInput").ap()
    b_h = {n: nc.dram_tensor(n, [1, D], F32, kind="ExternalInput").ap()
           for n in ("bq", "bk", "bv", "bo")}
    cx_h = nc.dram_tensor("clip_x", [1, 1], F32, kind="ExternalInput").ap()
    cd_h = nc.dram_tensor("clip_down", [1, 1], F32, kind="ExternalInput").ap()
    wg_h = nc.dram_tensor("w_gate", [DI, D], F32, kind="ExternalInput").ap()
    wu_h = nc.dram_tensor("w_up", [DI, D], F32, kind="ExternalInput").ap()
    wd_h = nc.dram_tensor("w_down", [D, DI], F32, kind="ExternalInput").ap()
    msk_h = nc.dram_tensor("masks", [5 * 128, 256], BF16, kind="ExternalInput").ap()
    fix_h = nc.dram_tensor("fixes", [128, 512], BF16, kind="ExternalInput").ap()
    fxl_h = nc.dram_tensor("fix_l", [1, 256], F32, kind="ExternalInput").ap()
    idn_h = nc.dram_tensor("ident", [128, 128], F32, kind="ExternalInput").ap()
    y_h = nc.dram_tensor("y", [SM, D], F32, kind="ExternalOutput").ap()

    xn_sc = nc.dram_tensor("xn_sc", [S, D], BF16, kind="Internal").ap()
    xnm_sc = nc.dram_tensor("xnm_sc", [SM, D], BF16, kind="Internal").ap()
    K_sc = nc.dram_tensor("K_sc", [NC, 128, S], BF16, kind="Internal").ap()
    V_sc = nc.dram_tensor("V_sc", [S, D], BF16, kind="Internal").ap()
    ao_sc = nc.dram_tensor("ao_sc", [D, SM], BF16, kind="Internal").ap()
    t_sc = nc.dram_tensor("t_sc", [NI, 128, SM], F32, kind="Internal").ap()
    wgq_sc = nc.dram_tensor("wgq_sc", [DI, D], BF16, kind="Internal").ap()
    wuq_sc = nc.dram_tensor("wuq_sc", [DI, D], BF16, kind="Internal").ap()
    wdq_sc = nc.dram_tensor("wdq_sc", [D, DI], BF16, kind="Internal").ap()
    ar1_in = nc.dram_tensor("ar1_in", [1, 16], F32, kind="Internal").ap()
    ar1_out = nc.dram_tensor("ar1_out", [1, 16], F32, kind="Internal").ap()
    ar2_in = nc.dram_tensor("ar2_in", [1, 16], F32, kind="Internal").ap()
    ar2_out = nc.dram_tensor("ar2_out", [1, 16], F32, kind="Internal").ap()

    A = nc.alloc_sbuf_tensor
    msk_sb = A("msk_sb", [128, 5 * 256], BF16)
    fix_sb = A("fix_sb", [128, 512], BF16)
    fxl_sb = A("fxl_sb", [1, 256], F32)
    idn_sb = A("idn_sb", [128, 128], F32)
    ones_c = A("ones_c", [128, 1], BF16)
    ones_r = A("ones_r", [1, 128], F32)
    bq_sb = A("bq_sb", [128, NC], F32)
    bk_sb = A("bk_sb", [128, NC], F32)
    bo_sb = A("bo_sb", [128, NC], F32)
    bv_row = A("bv_row", [1, D], F32)
    bv_rowb = A("bv_rowb", [1, D], BF16)
    ones_rb = A("ones_rb", [1, 128], BF16)
    cx_sb = A("cx_sb", [1, 1], F32)
    cd_sb = A("cd_sb", [1, 1], F32)
    gmax_run = A("gmax_run", [128, 1], F32)
    tmax_run = A("tmax_run", [128, 1], F32)
    gstat = A("gstat", [1, 16], F32)
    scl = A("scl", [1, 16], F32)
    sx_bc = A("sx_bc", [128, 1], F32)
    rx_bc = A("rx_bc", [128, 1], F32)
    sh_bc = A("sh_bc", [128, 1], F32)
    rh_bc = A("rh_bc", [128, 1], F32)
    vmean = A("vmean", [128, NC], F32)
    vsum_row = A("vsum_row", [1, D], F32)
    lnst = A("lnst", [128, 12 * 8], F32)
    stats_mr = A("stats_mr", [128, 64], F32)
    lmv = A("lmv", [128, 4 * 8], F32)
    wqs = A("wqs", [128, 4], F32)
    tstat = A("tstat", [128, 1], F32)
    l_raw = A("l_raw", [1, 256], F32)
    eps_sb = A("eps_sb", [128, 1], F32)
    linv_sb = A("linv_sb", [1, 256], F32)

    base0 = nc.sbuf_base
    SB_CAP = 212700
    KB = 1024

    def at(name, shape, dtype, off):
        esz = {F32: 4, BF16: 2, I32: 4}[dtype]
        assert base0 + off + shape[-1] * esz <= SB_CAP, (name, base0, off)
        return nc.alloc_sbuf_tensor_at(name, shape, dtype, offset=base0 + off)

    xA_t = at("xA0", [128, 4 * D], F32, 0)
    xA = [xA_t, xA_t]
    xfr = [at(f"xfr{i}", [128, NC * 512], F32, 16 * KB + i * 16 * KB) for i in range(2)]
    xfm = [at(f"xfm{i}", [128, NC * 512], BF16, 48 * KB + i * 8 * KB) for i in range(2)]
    kstage = [at(f"kstage{i}", [128, NC * 512], BF16, 64 * KB + i * 8 * KB) for i in range(2)]
    vstage = [at(f"vstage{i}", [128, 4 * D], BF16, 80 * KB + i * 8 * KB) for i in range(2)]
    scrm = at("scrm", [128, NC * 512], BF16, 96 * KB)
    scrsq = at("scrsq", [128, NC * 512], BF16, 104 * KB)
    mbc_t = at("mbc", [128, 2 * 512], F32, 112 * KB)
    mbc = [mbc_t, mbc_t]
    mrow_t = at("mrow", [1, 1024], F32, 116 * KB)
    mrow = [mrow_t, mrow_t]
    q_sb = [at(f"q_sb{i}", [128, NC * 256], BF16, i * 4 * KB) for i in range(2)]
    qxn = [at(f"qxn{i}", [128, NC * 256], BF16, 8 * KB + i * 4 * KB) for i in range(2)]
    kst = [at(f"kst{i}", [128, NC * 512], BF16, 16 * KB + i * 8 * KB) for i in range(2)]
    vst = [at(f"vst{i}", [128, 4 * D], BF16, 32 * KB + i * 8 * KB) for i in range(2)]
    exps = at("exps", [128, 4 * 256], BF16, 48 * KB)
    ao_st = [at(f"ao_st{i}", [128, NC * 256], BF16, 50 * KB + i * 4 * KB) for i in range(2)]
    wld = [at(f"wld{i}", [128, DI], F32, 58 * KB + i * 11 * KB) for i in range(2)]
    wqt = at("wqt", [128, DI], F32, 80 * KB)
    wqi = at("wqi", [128, DI], I32, 91 * KB)
    wqo = at("wqo", [128, DI], BF16, 102 * KB)
    onorm = at("onorm", [128, NC * 256], BF16, 107 * KB + 512)
    linv_bc = at("linv_bc", [128, 256], F32, 111 * KB + 512)
    wq_sb = at("wq_sb", [128, NC * D], BF16, 120 * KB)
    wk_sb = at("wk_sb", [128, NC * D], BF16, 136 * KB)
    wo_sb = at("wo_sb", [128, NC * D], BF16, 136 * KB)
    wv_sb = at("wv_sb", [128, NC * D], BF16, 152 * KB)
    wgT = at("wgT", [128, NC * DI], BF16, 0)
    wuT = at("wuT", [128, NC * DI], BF16, 44 * KB)
    FA = 88 * KB
    xq_sb = [at(f"xq_sb{i}", [128, NC * 512], BF16, FA + i * 8 * KB) for i in range(2)]
    g_sb = [at(f"g_sb{i}", [128, 512], F32, FA + 16 * KB + i * 2 * KB) for i in range(2)]
    t_sl = [at(f"t_sl{i}", [128, 512], F32, FA + 20 * KB + i * 2 * KB) for i in range(4)]
    xqt = [at(f"xqt{i}", [128, 512], F32, FA + 28 * KB + i * 2 * KB) for i in range(4)]
    xqi = [at(f"xqi{i}", [128, 512], I32, FA + 36 * KB + i * 2 * KB) for i in range(4)]
    wdT = at("wdT", [128, NI * 1024], BF16, 88 * KB)
    tB = [at(f"tB{i}", [128, NI * 256], F32, i * 22 * KB) for i in range(2)]
    hB = [at(f"hB{i}", [128, NI * 256], BF16, 44 * KB + i * 11 * KB) for i in range(2)]
    tBi = at("tBi", [128, NI * 256], I32, 66 * KB)
    FB = 132 * KB
    xB = [at(f"xB{i}", [128, 2 * D], F32, FB + i * 8 * KB) for i in range(2)]
    aoB = [at(f"aoB{i}", [128, D], BF16, FB + 16 * KB + i * 2 * KB) for i in range(4)]
    outB = [at(f"outB{i}", [128, D], F32, FB + 24 * KB + i * 4 * KB) for i in range(2)]

    P = nc.alloc_psum_tensor
    sc_ps = [P(f"sc{i}", [128, 512], F32) for i in range(2)]
    po_ps = [P(f"po{i}", [128, 512], F32) for i in range(4)]
    pl_ps = P("pl", [1, 1024], F32)

    E = Emitter(nc, [
        "c0", "wl", "wo", "ini",
        "lx0", "lx1", "xfl0", "xfl1", "st", "sq", "xn", "gm", "xc0", "xc1", "xf0", "xf1",
        "kpe", "kev", "vpe", "vev", "vsp", "sqc",
        "kspk0", "kspk1", "kspv0", "kspv1",
        "tp", "vme", "cc1", "cc2", "arx",
        "qx0", "qx1", "qpe", "qev", "kld0", "kld1", "vld0", "vld1",
        "sc", "ex", "mk", "pv", "epi", "bc2", "ope", "ob", "aos0", "aos1",
        "wfl0", "wfl1", "wqd", "wstg", "wstu", "wstd", "wt",
        "xqt0", "xqt1", "xqd", "xqa", "gu", "sil", "tt",
        "tst0", "tst1", "tst2", "tst3",
        "tp2", "sch",
        "tld0", "tld1", "xbl0", "xbl1", "aot0", "aot1", "aot2", "aot3", "hq", "hqa", "dn", "out",
        "yst0", "yst1",
    ])

    inv127 = float(np.float32(1.0) / np.float32(127.0))
    exp_scale = 1.0 / math.sqrt(float(D))

    def consts(e, sh):
        e.dma_start(out=msk_sb[:], in_=msk_h.rearrange("(m p) q -> p m q", p=128)).then_inc(sh, 16)
        e.dma_start(out=fix_sb[:], in_=fix_h[:]).then_inc(sh, 16)
        e.dma_start(out=fxl_sb[:], in_=fxl_h[:]).then_inc(sh, 16)
        e.dma_start(out=idn_sb[:], in_=idn_h[:]).then_inc(sh, 16)
        e.dma_start(out=cx_sb[:], in_=cx_h[:]).then_inc(sh, 16)
        e.dma_start(out=cd_sb[:], in_=cd_h[:]).then_inc(sh, 16)
        e.dma_start(out=bv_row[:], in_=b_h["bv"][:]).then_inc(sh, 16)
    E.op("sync", consts, inc="c0", k=16 * 7, self_inc=True)

    for c in range(NC):
        def bias3(e, sh, c=c):
            e.dma_start(out=bq_sb[:, c:c + 1], in_=b_h["bq"][0:1, c * 128:(c + 1) * 128].rearrange("o p -> p o")).then_inc(sh, 16)
            e.dma_start(out=bk_sb[:, c:c + 1], in_=b_h["bk"][0:1, c * 128:(c + 1) * 128].rearrange("o p -> p o")).then_inc(sh, 16)
            e.dma_start(out=bo_sb[:, c:c + 1], in_=b_h["bo"][0:1, c * 128:(c + 1) * 128].rearrange("o p -> p o")).then_inc(sh, 16)
        E.op("sync", bias3, inc="c0", k=48, self_inc=True)
    C0 = E.now("c0")

    def aw(e, sh):
        for w_sb, w_hh in ((wq_sb, wq_h), (wk_sb, wk_h), (wv_sb, wv_h)):
            e.dma_start(out=w_sb[:], in_=w_hh.rearrange("(c p) n -> p c n", p=128)).then_inc(sh, 16)
    E.op("gpsimd", aw, inc="wl", k=48, self_inc=True)

    nc.const_aps.register(F32, EPS, eps_sb[:, 0:1])

    def inis(e):
        e.memset(eps_sb[:], EPS)
        e.memset(gstat[:], 0.0)
        e.memset(ones_c[:], 1.0)
        e.memset(ones_r[:], 1.0)
        e.memset(ones_rb[:], 1.0)
        e.memset(gmax_run[:], 0.0)
        return e.memset(tmax_run[:], 0.0)
    E.op("gpsimd", inis, inc="ini")

    def bvcp(e):
        return e.tensor_copy(bv_rowb[:], bv_row[:])
    E.op("vector", bvcp, waits=[("c0", C0)], inc="vev")

    kq_free = [("c0", 0), ("c0", 0)]
    po_free = [("c0", 0), ("c0", 0), ("c0", 0), ("c0", 0)]
    mrow_free = [("c0", 0)]
    scrm_free = [("c0", 0), ("c0", 0)]
    scrsq_free = [("c0", 0)]
    rowmath_done = [0]
    sqc_val = [0]
    sq_val = [0]
    sqsv_of = {}
    xfm_free = {0: 0, 1: 0}
    xfm_built = {0: 0, 1: 0}
    xfr_free = {}
    xA_free = {}
    mbc_free = {0: 0, 1: 0}
    lnst_free = {}
    kstage_free = [("c0", 0), ("c0", 0)]
    vstage_free = [("c0", 0), ("c0", 0)]

    sched = []
    jm_next = 0
    for j in range(NT):
        sched.append(("A", j))
        if j % 2 == 1 and jm_next < NFB:
            sched.append(("B", jm_next))
            jm_next += 1
    while jm_next < NFB:
        sched.append(("B", jm_next))
        jm_next += 1

    def kv_stage(j, p2):
        xfv = xfm_built[p2]

        for o in range(NC):
            bank = o % 2
            E.op("tensor", lambda e, p2=p2, o=o, bank=bank: [
                 e.matmul(sc_ps[bank][:, 0:512],
                          wk_sb[:, c * D + o * 128:c * D + o * 128 + 128],
                          xfm[p2][:, 512 * c:512 * (c + 1)],
                          start=(c == 0), stop=(c == NC - 1)) for c in range(NC)][-1],
                 waits=[("xn", xfv), ("wl", 48), kq_free[bank]], inc="kpe")
            wv_ = [("kpe", E.now("kpe")), ("c0", C0)]
            if o == 0:
                wv_.append(kstage_free[p2])
            E.op("scalar", lambda e, p2=p2, o=o, bank=bank:
                 e.activation(out=kstage[p2][:, 512 * o:512 * (o + 1)], in_=sc_ps[bank][:, 0:512],
                              func=AF.Identity, bias=bk_sb[:, o:o + 1]),
                 waits=wv_, inc="kev")
            kq_free[bank] = ("kev", E.now("kev"))

        for v in range(8):
            tt_, dvb = v // 2, v % 2
            bank = v % 2

            def vmm(e, p2=p2, tt_=tt_, dvb=dvb, bank=bank):
                for c in range(NC):
                    e.matmul(po_ps[bank][:, 0:512],
                             xfm[p2][:, 512 * c + 128 * tt_:512 * c + 128 * tt_ + 128],
                             wv_sb[:, c * D + 512 * dvb:c * D + 512 * dvb + 512],
                             start=(c == 0), stop=False, skip_group_check=True)
                return e.matmul(po_ps[bank][:, 0:512], ones_rb[:],
                                bv_rowb[0:1, 512 * dvb:512 * dvb + 512],
                                start=False, stop=True, skip_group_check=True)
            E.op("tensor", vmm,
                 waits=[("xn", xfv), ("wl", 48), ("vev", 1), po_free[bank]], inc="vpe")
            wv_ = [("vpe", E.now("vpe"))]
            if v == 0:
                wv_.append(vstage_free[p2])
            E.op("scalar", lambda e, p2=p2, tt_=tt_, dvb=dvb, bank=bank:
                 e.activation(out=vstage[p2][:, 1024 * tt_ + 512 * dvb:1024 * tt_ + 512 * dvb + 512],
                              in_=po_ps[bank][:, 0:512], func=AF.Identity),
                 waits=wv_, inc="vev")
            po_free[bank] = ("vev", E.now("vev"))

        def vsum(e, j=j, p2=p2):
            for v in range(8):
                tt_, dvb = v // 2, v % 2
                ins = e.matmul(pl_ps[0:1, 512 * dvb:512 * dvb + 512], ones_c[:],
                               vstage[p2][:, 1024 * tt_ + 512 * dvb:1024 * tt_ + 512 * dvb + 512],
                               start=(j == 0 and v < 2), stop=(j == NT - 1 and v >= 6),
                               skip_group_check=True)
            return ins
        E.op("tensor", vsum, waits=[("vev", E.now("vev")), ("ini", 1)], inc="vsp")
        xfm_free[p2] = E.now("vsp")

        kss = "kspk%d" % p2
        E.op("gpsimd", lambda e, j=j, p2=p2:
             e.dma_start(out=K_sc.rearrange("o p s -> p o s")[:, :, 512 * j:512 * (j + 1)],
                         in_=kstage[p2][:]),
             waits=[("kev", E.now("kev"))], inc=kss, k=16)
        kstage_free[p2] = (kss, E.now(kss))
        vss = "kspv%d" % p2
        E.op("gpsimd", lambda e, j=j, p2=p2:
             e.dma_start(out=V_sc[512 * j:512 * (j + 1), :].rearrange("(k p) d -> p k d", p=128),
                         in_=vstage[p2][:]),
             waits=[("vev", E.now("vev"))], inc=vss, k=16)
        vstage_free[p2] = (vss, E.now(vss))

    xfl_val = {}

    def emit_ld(k):
        p = k % 2
        jj = a_js[k]
        E.op("sync", lambda e, p=p, jj=jj:
             e.dma_start(out=xfr[p][:], in_=xtf[:, 512 * jj:512 * (jj + 1)]
                         .rearrange("(c p) s -> p c s", p=128)),
             waits=xfr_free.get(p, []), inc="xfl%d" % p, k=16)
        xfl_val[p] = E.now("xfl%d" % p)

    def a_scrm(k):
        jj = a_js[k]
        E.op("gpsimd", lambda e, jj=jj:
             e.dma_start(out=scrm[:], in_=xtf[:, 512 * jj:512 * (jj + 1)]
                         .rearrange("(c p) s -> p c s", p=128)),
             waits=list(scrm_free), inc="sqc", k=16)
        sqc_val[0] = E.now("sqc")

    def a_squares(k):
        for c in range(NC):
            w0 = [("sqc", sqc_val[0]), scrsq_free[0]] if c == 0 else []
            E.op("scalar", lambda e, c=c:
                 e.activation(out=scrsq[:, 512 * c:512 * (c + 1)],
                              in_=scrm[:, 512 * c:512 * (c + 1)], func=AF.Square),
                 waits=w0, inc="sq")
        sq_val[0] = E.now("sq")

    def a_sums(k):
        def msum(e):
            for c in range(NC):
                ins = e.matmul(po_ps[2][0:1, 0:512], ones_c[:],
                               scrm[:, 512 * c:512 * (c + 1)],
                               start=(c == 0), stop=(c == NC - 1),
                               skip_group_check=True)
            return ins
        E.op("tensor", msum, waits=[("sqc", sqc_val[0]), po_free[2], ("ini", 1)], inc="tp")
        scrm_free[0] = ("tp", E.now("tp"))
        scrm_free[1] = ("sq", sq_val[0])

        def sqsum(e):
            for c in range(NC):
                ins = e.matmul(po_ps[3][0:1, 0:512], ones_c[:],
                               scrsq[:, 512 * c:512 * (c + 1)],
                               start=(c == 0), stop=(c == NC - 1),
                               skip_group_check=True)
            return ins
        E.op("tensor", sqsum, waits=[("sq", sq_val[0]), po_free[3]], inc="tp")
        scrsq_free[0] = ("tp", E.now("tp"))
        sqsv_of[k] = E.now("tp")

    def a_rowmath(k):
        sqsv = sqsv_of[k]
        p = k % 2
        E.op("vector", lambda e, p=p:
             e.tensor_scalar(mrow[p][0:1, 0:512], po_ps[2][0:1, 0:512],
                             1.0 / float(D), None, ALU.mult),
             waits=[("tp", sqsv), mrow_free[0]], inc="st")
        E.op("vector", lambda e, p=p:
             e.tensor_scalar(mrow[p][0:1, 512:1024], po_ps[3][0:1, 0:512],
                             1.0 / float(D), None, ALU.mult),
             waits=[], inc="st")
        E.op("vector", lambda e, p=p:
             e.tensor_tensor(po_ps[2][0:1, 0:512], mrow[p][0:1, 0:512],
                             mrow[p][0:1, 0:512], ALU.mult),
             waits=[("st", E.now("st"))], inc="st")
        E.op("vector", lambda e, p=p:
             e.tensor_tensor(mrow[p][0:1, 512:1024], mrow[p][0:1, 512:1024],
                             po_ps[2][0:1, 0:512], ALU.subtract),
             waits=[("st", E.now("st"))], inc="st")
        E.op("scalar", lambda e, p=p:
             e.activation(out=po_ps[2][0:1, 0:512], in_=mrow[p][0:1, 512:1024],
                          func=AF.Sqrt, bias=EPS),
             waits=[("st", E.now("st"))], inc="sq")
        E.op("vector", lambda e, p=p:
             e.reciprocal(mrow[p][0:1, 512:1024], po_ps[2][0:1, 0:512]),
             waits=[("sq", E.now("sq"))], inc="st")
        rowmath_done[0] = E.now("st")

    def a_norm(k, j):
        p = k % 2
        s0 = 4 * p

        def bcm2(e, p=p):
            e.matmul(po_ps[2][:, 0:512], ones_r[:], mrow[p][0:1, 0:512],
                     start=True, stop=True)
            return e.matmul(po_ps[3][:, 0:512], ones_r[:], mrow[p][0:1, 512:1024],
                            start=True, stop=True)
        E.op("tensor", bcm2, waits=[("st", rowmath_done[0])], inc="tp")
        mrow_free[0] = ("tp", E.now("tp"))
        E.op("vector", lambda e, p=p:
             e.tensor_copy(mbc[p][:, 0:512], po_ps[2][:, 0:512]),
             waits=[("tp", E.now("tp")), ("xn", max(mbc_free.values()))], inc="st")
        po_free[2] = ("st", E.now("st"))
        E.op("vector", lambda e, p=p:
             e.tensor_copy(mbc[p][:, 512:1024], po_ps[3][:, 0:512]),
             waits=[], inc="st")
        po_free[3] = ("st", E.now("st"))
        bcv = E.now("st")
        for c in range(NC):
            w0 = []
            if c == 0:
                w0 = [("st", bcv), ("vsp", xfm_free.get(p, 0)),
                      ("xfl%d" % p, xfl_val[p])]
            E.op("vector", lambda e, p=p, c=c:
                 e.tensor_tensor(xfr[p][:, 512 * c:512 * (c + 1)],
                                 xfr[p][:, 512 * c:512 * (c + 1)], mbc[p][:, 0:512],
                                 ALU.subtract),
                 waits=w0, inc="xn")
            E.op("vector", lambda e, p=p, c=c:
                 e.tensor_mul(xfm[p][:, 512 * c:512 * (c + 1)],
                              xfr[p][:, 512 * c:512 * (c + 1)], mbc[p][:, 512:1024]),
                 waits=[("xn", E.now("xn"))], inc="xn")
        xfr_free[p] = [("xn", E.now("xn"))]
        mbc_free[p] = E.now("xn")
        E.op("vector", lambda e, p=p, s0=s0:
             e.reduce_max(lmv[:, 4 * s0 + 3:4 * s0 + 4], xfm[p][:], AX.X,
                          apply_absolute_value=True),
             waits=[("xn", E.now("xn")), ("ini", 1)], inc="st")
        E.op("vector", lambda e, s0=s0:
             e.tensor_max(gmax_run[:], gmax_run[:], lmv[:, 4 * s0 + 3:4 * s0 + 4]),
             waits=[("st", E.now("st"))], inc="gm")

    def b_block(m, j, part=2):
        p2 = m % 2
        lxs = "lx%d" % p2
        xcs = "xc%d" % p2
        s0 = 4 * p2
        if part == 1:
            b_state[0] = (m, j)
        waits = [("st", lnst_free.get(p2, 0)), ("xn", xA_free.get("a", 0)),
                 ("xc0", E.now("xc0")), ("xc1", E.now("xc1"))]
        E.op("sync", lambda e, jm=j: e.dma_start(
             out=xA[0][:], in_=xm[512 * jm:512 * (jm + 1), :].rearrange("(k p) d -> p k d", p=128)),
             waits=waits, inc=lxs, k=16)
        lxv = E.now(lxs)
        for tt_ in range(4):
            slot = s0 + tt_
            xap = xA[0][:, D * tt_:D * (tt_ + 1)]
            E.op("vector", lambda e, slot=slot, xap=xap: (
                 e.bn_stats(lnst[:, 12 * slot:12 * slot + 6], xap[:, 0:512]),
                 e.bn_stats(lnst[:, 12 * slot + 6:12 * slot + 12], xap[:, 512:1024]))[-1],
                 waits=[(lxs, lxv)] if tt_ == 0 else [], inc="st")
        for tt_ in range(4):
            slot = s0 + tt_
            E.op("vector", lambda e, slot=slot:
                 e.bn_aggr(lmv[:, 4 * slot:4 * slot + 2], lnst[:, 12 * slot:12 * slot + 12]),
                 waits=[("st", E.now("st"))] if tt_ == 0 else [], inc="st")
        lnst_free[p2] = E.now("st")
        if part == 1:
            return
        b_back(m, j)

    def b_back(m, j):
        p2 = m % 2
        xcs = "xc%d" % p2
        s0 = 4 * p2
        E.op("scalar", lambda e, s0=s0:
             e.activation(out=lmv[:, 4 * s0 + 2:4 * s0 + 15:4],
                          in_=lmv[:, 4 * s0 + 1:4 * s0 + 14:4], func=AF.Sqrt, bias=EPS),
             waits=[("st", E.now("st")), ("ini", 1)], inc="sq")
        E.op("vector", lambda e, s0=s0:
             e.reciprocal(lmv[:, 4 * s0 + 1:4 * s0 + 14:4], lmv[:, 4 * s0 + 2:4 * s0 + 15:4]),
             waits=[("sq", E.now("sq"))], inc="st")
        stv = E.now("st")
        E.op("vector", lambda e, s0=s0, j=j: (
             e.tensor_copy(stats_mr[:, 4 * j:4 * j + 4], lmv[:, 4 * s0:4 * s0 + 16:4]),
             e.tensor_copy(stats_mr[:, 32 + 4 * j:32 + 4 * j + 4],
                           lmv[:, 4 * s0 + 1:4 * s0 + 16:4]))[-1],
             waits=[("st", stv)], inc="st")
        for tt_ in range(4):
            slot = s0 + tt_
            xap = xA[0][:, D * tt_:D * (tt_ + 1)]
            E.op("vector", lambda e, slot=slot, xap=xap:
                 e.tensor_scalar(xap, xap, lmv[:, 4 * slot:4 * slot + 1],
                                 lmv[:, 4 * slot + 1:4 * slot + 2], ALU.subtract, ALU.mult),
                 waits=[], inc="xn")
        xA_free["a"] = E.now("xn")
        E.op("gpsimd", lambda e, jm=j: e.dma_start(
             out=xnm_sc[512 * jm:512 * (jm + 1), :].rearrange("(k p) d -> p k d", p=128),
             in_=xA[0][:]),
             waits=[("xn", E.now("xn"))], inc=xcs, k=16)

    a_js = [j for kind, j in sched if kind == "A"]
    aidx = 0
    bidx = 0
    kv_pend = []
    NA = len(a_js)
    b_state = [None]
    sched_a = [(i, kj) for i, kj in enumerate(sched) if kj[0] == "A"]
    b_after = {}
    for i, (kind, j) in enumerate(sched):
        if kind == "B":
            prev_a = max(ai for ai, (si, _) in enumerate(sched_a) if si < i)
            b_after[prev_a] = j
    for kidx, (si, (kind, j)) in enumerate(sched_a):
        k = aidx
        if k == 0:
            emit_ld(0)
            a_scrm(0)
            a_squares(0)
        if k + 1 < NA:
            emit_ld(k + 1)
        a_sums(k)
        if k + 1 < NA:
            a_scrm(k + 1)
        if k in b_after:
            b_block(bidx, b_after[k], part=1)
        if len(kv_pend) == 2:
            kv_stage(*kv_pend.pop(0))
        a_rowmath(k)
        if k in b_after:
            b_back(bidx, b_after[k])
            bidx += 1
        if k + 1 < NA:
            a_squares(k + 1)
        a_norm(k, j)
        xfm_built[k % 2] = E.now("xn")
        kv_pend.append((j, k % 2))
        aidx += 1
    while kv_pend:
        kv_stage(*kv_pend.pop(0))

    P1_XN = E.now("xn")
    P1_VSP = E.now("vsp")
    P1_SPILLS = [(n, E.now(n)) for n in ("kspk0", "kspk1", "kspv0", "kspv1")]
    P1_XCS = [(n, E.now(n)) for n in ("xc0", "xc1")]
    P1_GM = E.now("gm")

    E.op("tensor", lambda e: e.transpose(sc_ps[0][0:1, 0:128], gmax_run[:], idn_sb[:]),
         waits=[("gm", E.now("gm")), kq_free[0]], inc="tp")
    E.op("vector", lambda e: e.reduce_max(gstat[0:1, 0:1], sc_ps[0][0:1, 0:128], AX.X),
         waits=[("tp", E.now("tp"))], inc="gm")
    kq_free[0] = ("gm", E.now("gm"))
    E.op("sync", lambda e: e.dma_start(out=ar1_in[:], in_=gstat[:]),
         waits=[("gm", E.now("gm")), ("ini", 1)], inc="arx", k=16)
    E.op("gpsimd", lambda e: e.collective_compute(
        "AllReduce", ALU.max, replica_groups=[[0, 1, 2, 3, 4, 5, 6, 7]],
        ins=[ar1_in[:]], outs=[ar1_out[:]]),
        waits=[("arx", E.now("arx"))], inc="cc1")

    E.op("vector", lambda e: e.tensor_copy(vsum_row[:], pl_ps[0:1, 0:D]),
         waits=[("vsp", E.now("vsp"))], inc="vme")
    pl_free = ("vme", E.now("vme"))
    E.op("tensor", lambda e: [e.transpose(sc_ps[1][:, o:o + 1],
                                          vsum_row[0:1, 128 * o:128 * (o + 1)], idn_sb[0:1, 0:1])
                              for o in range(NC)][-1],
         waits=[("vme", E.now("vme")), kq_free[1]], inc="tp")
    E.op("vector", lambda e: e.tensor_scalar(vmean[:], sc_ps[1][:, 0:NC], 1.0 / float(S), None, ALU.mult),
         waits=[("tp", E.now("tp"))], inc="vme")
    kq_free[1] = ("vme", E.now("vme"))

    E.op("gpsimd", lambda e: e.dma_start(out=wo_sb[:], in_=wo_h.rearrange("(c p) n -> p c n", p=128)),
         waits=[("kpe", E.now("kpe"))] + P1_SPILLS, inc="wo", k=16)

    P1D = [("xn", P1_XN), ("vsp", P1_VSP), ("gm", P1_GM)] + P1_SPILLS + P1_XCS

    wld_free = [("c0", 0), ("c0", 0)]
    wqo_free = [("c0", 0)]
    first_wprep = [True]

    def wprep_chunk(job_idx, src_h, dst_sc, row0, ncols, wsts):
        sl = job_idx % 2
        wfs = "wfl%d" % sl
        w0 = [wld_free[sl]] + (P1D if first_wprep[0] else [])
        E.op("sync", lambda e, sl=sl, src_h=src_h, row0=row0, ncols=ncols:
             e.dma_start(out=wld[sl][:, 0:ncols], in_=src_h[row0:row0 + 128, 0:ncols]),
             waits=w0, inc=wfs, k=16)
        first_wprep[0] = False
        E.op("vector", lambda e, sl=sl, ncols=ncols:
             e.reduce_max(wqs[:, 0:1], wld[sl][:, 0:ncols], AX.X, apply_absolute_value=True),
             waits=[(wfs, E.now(wfs))], inc="wqd")
        E.op("vector", lambda e:
             e.tensor_scalar(wqs[:, 1:2], wqs[:, 0:1], inv127, None, ALU.mult),
             waits=[("wqd", E.now("wqd"))], inc="wqd")
        E.op("vector", lambda e: e.reciprocal(wqs[:, 2:3], wqs[:, 1:2]),
             waits=[("wqd", E.now("wqd"))], inc="wqd")
        E.op("vector", lambda e, sl=sl, ncols=ncols:
             e.tensor_scalar(wqt[:, 0:ncols], wld[sl][:, 0:ncols], wqs[:, 2:3], None, ALU.mult),
             waits=[("wqd", E.now("wqd"))], inc="wqd")
        wld_free[sl] = ("wqd", E.now("wqd"))
        E.op("scalar", lambda e, ncols=ncols:
             e.activation(out=wqi[:, 0:ncols], in_=wqt[:, 0:ncols], func=AF.Identity),
             waits=[("wqd", E.now("wqd"))], inc="wqd")
        E.op("scalar", lambda e, ncols=ncols:
             e.activation(out=wqt[:, 0:ncols], in_=wqi[:, 0:ncols], func=AF.Identity),
             waits=[("wqd", E.now("wqd"))], inc="wqd")
        E.op("vector", lambda e, ncols=ncols:
             e.tensor_scalar(wqo[:, 0:ncols], wqt[:, 0:ncols], wqs[:, 1:2], None, ALU.mult),
             waits=[("wqd", E.now("wqd")), wqo_free[0]], inc="wqd")
        E.op("gpsimd", lambda e, dst_sc=dst_sc, row0=row0, ncols=ncols:
             e.dma_start(out=dst_sc[row0:row0 + 128, 0:ncols], in_=wqo[:, 0:ncols]),
             waits=[("wqd", E.now("wqd"))], inc=wsts, k=16)
        wqo_free[0] = (wsts, E.now(wsts))

    wprep_jobs = ([(wg_h, wgq_sc, 128 * r, D, "wstg") for r in range(NI)]
                  + [(wu_h, wuq_sc, 128 * r, D, "wstu") for r in range(NI)]
                  + [(wd_h, wdq_sc, 128 * r, DI, "wstd") for r in range(NC)])
    job_idx = [0]

    per_u_box = [0]

    def emit_jobs(count):
        while count > 0 and job_idx[0] < len(wprep_jobs):
            src_h, dst_sc, row0, ncols, wsts = wprep_jobs[job_idx[0]]
            wprep_chunk(job_idx[0], src_h, dst_sc, row0, ncols, wsts)
            job_idx[0] += 1
            count -= 1

    qxn_free = [("c0", 0), ("c0", 0)]
    kst_free = [("c0", 0), ("c0", 0)]
    vst_free = [("c0", 0), ("c0", 0)]
    exp_free = [("c0", 0)] * 4
    ao_free = [("c0", 0), ("c0", 0)]
    q_free = [("c0", 0), ("c0", 0)]
    onorm_free = [("c0", 0)]
    pl_free_box = [pl_free]
    first_p2 = {"qxt": True, "qev": True, "exp": True, "oev": True, "mk": True}

    qev_of = {}

    def qproj(u):
        up = u % 2
        qxs = "qx%d" % up
        w0 = P1_XCS + [qxn_free[up]] + (P1D if first_p2["qxt"] else [])
        first_p2["qxt"] = False
        E.op("sync", lambda e, sh, u=u, up=up: [
             e.dma_start(out=qxn[up][:, 256 * c:256 * (c + 1)],
                         in_=xnm_sc[256 * u:256 * (u + 1), 128 * c:128 * (c + 1)],
                         transpose=True).then_inc(sh, 16) for c in range(NC)][-1],
             waits=w0, inc=qxs, k=16 * NC, self_inc=True)
        qxv = E.now(qxs)
        for o in range(NC):
            bank = o % 2
            E.op("tensor", lambda e, up=up, o=o, bank=bank: [
                 e.matmul(sc_ps[bank][:, 0:256],
                          wq_sb[:, c * D + o * 128:c * D + o * 128 + 128],
                          qxn[up][:, 256 * c:256 * (c + 1)],
                          start=(c == 0), stop=(c == NC - 1)) for c in range(NC)][-1],
                 waits=[(qxs, qxv), kq_free[bank]], inc="qpe")
            wv_ = [("qpe", E.now("qpe"))]
            if o == 0:
                wv_.append(q_free[up])
                if first_p2["qev"]:
                    wv_ += P1D
                    first_p2["qev"] = False
            E.op("scalar", lambda e, up=up, o=o, bank=bank:
                 e.activation(out=q_sb[up][:, 256 * o:256 * (o + 1)], in_=sc_ps[bank][:, 0:256],
                              func=AF.Identity, bias=bq_sb[:, o:o + 1]),
                 waits=wv_, inc="qev")
            kq_free[bank] = ("qev", E.now("qev"))
        qxn_free[up] = ("qpe", E.now("qpe"))
        qev_of[u] = E.now("qev")

    def p2_supertile(u):
        up = u % 2
        n_i = NB - 4 * u
        qev_v = qev_of[u]

        jobs_at = max(2, (3 * (n_i + 2)) // 4)
        pv_pend = {}
        for step in range(n_i + 2):
            if step == jobs_at:
                emit_jobs(per_u_box[0])
            if step < n_i:
                i = step
                il = i % 4
                gg = u + i // 4
                buf = gg % 2
                klds = "kld%d" % buf
                vlds = "vld%d" % buf
                if il == 0:
                    E.op("sync", lambda e, gg=gg, buf=buf:
                         e.dma_start(out=kst[buf][:],
                                     in_=K_sc.rearrange("o p s -> p o s")[:, :, 512 * gg:512 * (gg + 1)]),
                         waits=P1_SPILLS + [kst_free[buf]], inc=klds, k=16)
                    E.op("gpsimd", lambda e, gg=gg, buf=buf:
                         e.dma_start(out=vst[buf][:],
                                     in_=V_sc[512 * gg:512 * (gg + 1), :].rearrange("(k p) d -> p k d", p=128)),
                         waits=P1_SPILLS + [vst_free[buf]], inc=vlds, k=16)
                kldv, vldv = E.now(klds), E.now(vlds)
                sbank = i % 2
                slot = i % 4

                E.op("tensor", lambda e, up=up, i=i, il=il, buf=buf, sbank=sbank: [
                     e.matmul(sc_ps[sbank][:, 0:256],
                              kst[buf][:, 512 * c + 128 * il:512 * c + 128 * il + 128],
                              q_sb[up][:, 256 * c:256 * (c + 1)],
                              start=(c == 0), stop=(c == NC - 1)) for c in range(NC)][-1],
                     waits=[(klds, kldv), ("qev", qev_v), kq_free[sbank]], inc="sc")
                if il == 3 or i == n_i - 1:
                    kst_free[buf] = ("sc", E.now("sc"))

                w0 = [("sc", E.now("sc")), exp_free[slot]] + (P1D if first_p2["exp"] else [])
                first_p2["exp"] = False
                E.op("scalar", lambda e, slot=slot, sbank=sbank:
                     e.activation(out=exps[:, 256 * slot:256 * (slot + 1)],
                                  in_=sc_ps[sbank][:, 0:256], func=AF.Exp, scale=exp_scale),
                     waits=w0, inc="ex")
                kq_free[sbank] = ("ex", E.now("ex"))

                if i <= 4:
                    w0 = [("ex", E.now("ex")), ("c0", C0)] + (P1D if first_p2["mk"] else [])
                    first_p2["mk"] = False
                    E.op("vector", lambda e, i=i, slot=slot:
                         e.tensor_mul(exps[:, 256 * slot:256 * (slot + 1)],
                                      exps[:, 256 * slot:256 * (slot + 1)],
                                      msk_sb[:, 256 * i:256 * (i + 1)]),
                         waits=w0, inc="mk")
                    pv_pend[i] = (il, buf, slot, ("mk", E.now("mk")), vlds, vldv)
                else:
                    pv_pend[i] = (il, buf, slot, ("ex", E.now("ex")), vlds, vldv)

            if step >= 2:
                i = step - 2
                il, buf, slot, pv_wait, vlds, vldv = pv_pend.pop(i)
                wv_ = [pv_wait, (vlds, vldv), ("ini", 1)]
                if i == 0:
                    wv_ += [po_free[0], po_free[1], po_free[2], po_free[3], pl_free_box[0]]

                def pvf(e, i=i, il=il, buf=buf, slot=slot, n_i=n_i):
                    for m in range(NC):
                        e.matmul(po_ps[m // 2][:, 256 * (m % 2):256 * (m % 2) + 256],
                                 vst[buf][:, 1024 * il + 128 * m:1024 * il + 128 * m + 128],
                                 exps[:, 256 * slot:256 * (slot + 1)],
                                 start=(i == 0 and (m % 2 == 0)), stop=(i == n_i - 1),
                                 skip_group_check=True)
                    return e.matmul(pl_ps[0:1, 0:256], ones_c[:], exps[:, 256 * slot:256 * (slot + 1)],
                                    start=(i == 0), stop=(i == n_i - 1), skip_group_check=True)
                E.op("tensor", pvf, waits=wv_, inc="pv")
                exp_free[slot] = ("pv", E.now("pv"))
                if il == 3 or i == n_i - 1:
                    vst_free[buf] = ("pv", E.now("pv"))

        if u == NU - 1:
            E.op("vector", lambda e: e.tensor_add(l_raw[:], pl_ps[0:1, 0:256], fxl_sb[:]),
                 waits=[("pv", E.now("pv")), ("c0", C0)], inc="epi")
        else:
            E.op("vector", lambda e: e.tensor_copy(l_raw[:], pl_ps[0:1, 0:256]),
                 waits=[("pv", E.now("pv"))], inc="epi")
        E.op("vector", lambda e: e.reciprocal(linv_sb[:], l_raw[:]),
             waits=[("epi", E.now("epi"))], inc="epi")
        pl_free_box[0] = ("epi", E.now("epi"))
        E.op("tensor", lambda e: e.matmul(sc_ps[0][:, 0:256], ones_r[:], linv_sb[:],
                                          start=True, stop=True),
             waits=[("epi", E.now("epi")), kq_free[0]], inc="bc2")
        E.op("vector", lambda e: e.tensor_copy(linv_bc[:], sc_ps[0][:, 0:256]),
             waits=[("bc2", E.now("bc2"))], inc="epi")
        kq_free[0] = ("epi", E.now("epi"))

        if u + 1 < NU:
            qproj(u + 1)

        for m in range(NC):
            wv_ = [("epi", E.now("epi"))]
            if m == 0:
                wv_.append(onorm_free[0])
            E.op("vector", lambda e, m=m:
                 e.tensor_mul(onorm[:, 256 * m:256 * (m + 1)],
                              po_ps[m // 2][:, 256 * (m % 2):256 * (m % 2) + 256], linv_bc[:]),
                 waits=wv_, inc="epi")
        for b_ in range(4):
            po_free[b_] = ("epi", E.now("epi"))

        if u == NU - 1:
            for m in range(NC):
                E.op("vector", lambda e, m=m:
                     e.tensor_mul(onorm[:, 256 * m:256 * (m + 1)],
                                  onorm[:, 256 * m:256 * (m + 1)], fix_sb[:, 0:256]),
                     waits=[("epi", E.now("epi")), ("vme", E.now("vme"))], inc="epi")
                E.op("vector", lambda e, m=m:
                     e.scalar_tensor_tensor(onorm[:, 256 * m:256 * (m + 1)],
                                            fix_sb[:, 256:512], vmean[:, m:m + 1],
                                            onorm[:, 256 * m:256 * (m + 1)], ALU.mult, ALU.add),
                     waits=[("epi", E.now("epi"))], inc="epi")

        for o in range(NC):
            bank = o % 2
            E.op("tensor", lambda e, o=o, bank=bank: [
                 e.matmul(sc_ps[bank][:, 0:256],
                          wo_sb[:, m * D + o * 128:m * D + o * 128 + 128],
                          onorm[:, 256 * m:256 * (m + 1)],
                          start=(m == 0), stop=(m == NC - 1)) for m in range(NC)][-1],
                 waits=[("epi", E.now("epi")), ("wo", 16), kq_free[bank]], inc="ope")
            wv_ = [("ope", E.now("ope"))]
            if o == 0:
                wv_.append(ao_free[up])
                if first_p2["oev"]:
                    wv_ += P1D
                    first_p2["oev"] = False
            E.op("scalar", lambda e, up=up, o=o, bank=bank:
                 e.activation(out=ao_st[up][:, 256 * o:256 * (o + 1)], in_=sc_ps[bank][:, 0:256],
                              func=AF.Identity, bias=bo_sb[:, o:o + 1]),
                 waits=wv_, inc="ob")
            kq_free[bank] = ("ob", E.now("ob"))
        onorm_free[0] = ("ope", E.now("ope"))

        aoss = "aos%d" % up
        E.op("sync", lambda e, u=u, up=up:
             e.dma_start(out=ao_sc.rearrange("(o p) q -> p o q", p=128)[:, :, 256 * u:256 * (u + 1)],
                         in_=ao_st[up][:]),
             waits=[("ob", E.now("ob"))], inc=aoss, k=16)
        ao_free[up] = (aoss, E.now(aoss))
        q_free[up] = ("sc", E.now("sc"))

    per_u_box[0] = int(math.ceil(len(wprep_jobs) / max(1, NU - 2)))
    qproj(0)
    for u in range(NU):
        p2_supertile(u)
    emit_jobs(len(wprep_jobs))
    P2_OPE = E.now("ope")
    P2_OB = E.now("ob")
    P2_AOS = [("aos0", E.now("aos0")), ("aos1", E.now("aos1"))]
    P2_PV = E.now("pv")
    P2_SC = E.now("sc")
    P2_QPE = E.now("qpe")

    def scale_chain(ar_out_ap, clip_sb, s_bc, r_bc, cc_sem, cc_val):
        E.op("sync", lambda e: e.dma_start(out=gstat[0:1, 8:16], in_=ar_out_ap[0:1, 0:8]),
             waits=[(cc_sem, cc_val)], inc="arx", k=16)
        arv = E.now("arx")
        E.op("scalar", lambda e: e.activation(out=scl[0:1, 0:1], in_=clip_sb[:], func=AF.Sigmoid),
             waits=[("c0", C0), ("sch", E.now("sch"))], inc="sch")
        E.op("vector", lambda e: e.tensor_mul(scl[0:1, 1:2], scl[0:1, 0:1], gstat[0:1, 8:9]),
             waits=[("sch", E.now("sch")), ("arx", arv)], inc="sch")
        E.op("vector", lambda e: e.tensor_scalar(scl[0:1, 2:3], scl[0:1, 1:2], inv127, None, ALU.mult),
             waits=[("sch", E.now("sch"))], inc="sch")
        E.op("vector", lambda e: e.reciprocal(scl[0:1, 3:4], scl[0:1, 2:3]),
             waits=[("sch", E.now("sch"))], inc="sch")
        E.op("tensor", lambda e: (
            e.matmul(sc_ps[0][:, 256:257], ones_r[:], scl[0:1, 2:3], start=True, stop=True),
            e.matmul(sc_ps[0][:, 257:258], ones_r[:], scl[0:1, 3:4], start=True, stop=True))[-1],
            waits=[("sch", E.now("sch")), kq_free[0]], inc="tp2")
        E.op("vector", lambda e, s_bc=s_bc, r_bc=r_bc: (
            e.tensor_copy(s_bc[:], sc_ps[0][:, 256:257]),
            e.tensor_copy(r_bc[:], sc_ps[0][:, 257:258]))[-1],
            waits=[("tp2", E.now("tp2"))], inc="sch")
        kq_free[0] = ("sch", E.now("sch"))
        return E.now("sch")

    sx_ready = scale_chain(ar1_out, cx_sb, sx_bc, rx_bc, "cc1", E.now("cc1"))

    P2D = ([("sc", P2_SC), ("pv", P2_PV), ("qpe", P2_QPE), ("ope", P2_OPE),
            ("ob", P2_OB), ("wqd", E.now("wqd")), ("ex", E.now("ex")),
            ("epi", E.now("epi")), ("mk", E.now("mk")), ("bc2", E.now("bc2")),
            ("wstg", 16 * NI), ("wstu", 16 * NI), ("wstd", 16 * NC)]
           + P2_AOS + [("wfl0", E.now("wfl0")), ("wfl1", E.now("wfl1"))])
    E.op("sync", lambda e, sh: [e.dma_start(out=wgT[:, DI * hc:DI * (hc + 1)],
                                        in_=wgq_sc[:, 128 * hc:128 * (hc + 1)], transpose=True).then_inc(sh, 16)
                            for hc in range(NC)][-1],
         waits=P2D + P1D + [("gm", E.now("gm"))], inc="wt", k=16 * NC, self_inc=True)
    E.op("sync", lambda e, sh: [e.dma_start(out=wuT[:, DI * hc:DI * (hc + 1)],
                                        in_=wuq_sc[:, 128 * hc:128 * (hc + 1)], transpose=True).then_inc(sh, 16)
                            for hc in range(NC)][-1],
         waits=[("wstu", 16 * NI)], inc="wt", k=16 * NC, self_inc=True)
    wt_gu = E.now("wt")

    xq_free = [("c0", 0), ("c0", 0)]
    xqt_free = [("c0", 0)] * 4
    t_free = [("c0", 0)] * 4
    g_free = [("c0", 0), ("c0", 0)]
    first_fa = [True]

    xqd_of = {}
    xq_ctx = {}

    def xq_load(f):
        fp = f % 2
        w0 = list(P1_XCS) + [xq_free[fp]]
        if first_fa[0]:
            w0 += [("ope", P2_OPE), ("ob", P2_OB), ("qpe", P2_QPE)]
            first_fa[0] = False
        E.op("sync", lambda e, sh, f=f, fp=fp: [
             e.dma_start(out=xq_sb[fp][:, 512 * c:512 * (c + 1)],
                         in_=xnm_sc[512 * f:512 * (f + 1), 128 * c:128 * (c + 1)],
                         transpose=True).then_inc(sh, 16) for c in range(NC)][-1],
             waits=w0, inc="xqt%d" % fp, k=16 * NC, self_inc=True)
        xq_ctx[f] = ("xqt%d" % fp, E.now("xqt%d" % fp), {})

    def q_stage1(f, c):
        fp = f % 2
        xqts, xqtv, st2q = xq_ctx[f]
        cs = c % 4
        E.op("scalar", lambda e, fp=fp, c=c, cs=cs:
             e.activation(out=xqi[cs][:], in_=xq_sb[fp][:, 512 * c:512 * (c + 1)],
                          func=AF.Identity, scale=rx_bc[:, 0:1]),
             waits=[(xqts, xqtv), ("sch", sx_ready), xqt_free[cs]], inc="xqa")
        E.op("vector", lambda e, cs=cs:
             e.tensor_scalar(xqi[cs][:], xqi[cs][:], -QMAX - 1.0, QMAX, ALU.max, ALU.min),
             waits=[("xqa", E.now("xqa"))], inc="xqd")
        st2q[c] = E.now("xqd")

    def q_stage2(f, c):
        fp = f % 2
        st2q = xq_ctx[f][2]
        cs = c % 4
        E.op("scalar", lambda e, fp=fp, c=c, cs=cs:
             e.activation(out=xq_sb[fp][:, 512 * c:512 * (c + 1)], in_=xqi[cs][:],
                          func=AF.Identity, scale=sx_bc[:, 0:1]),
             waits=[("xqd", st2q[c])], inc="xqa")
        xqt_free[cs] = ("xqa", E.now("xqa"))

    def xq_wave(f, c):
        q_stage1(f, c)
        if c >= 3:
            q_stage2(f, c - 3)
        if c == NC - 1:
            for cc in range(NC - 3, NC):
                q_stage2(f, cc)
            xqd_of[f] = E.now("xqa")

    xq_load(0)
    for c in range(NC):
        xq_wave(0, c)
    for f in range(NF):
        fp = f % 2
        xqdv = xqd_of[f]

        for ic in range(NI):
            if f + 1 < NF:
                if ic == 4:
                    xq_load(f + 1)
                elif ic >= 6 and ic <= 20 and ic % 2 == 0:
                    xq_wave(f + 1, (ic - 6) // 2)
            gb = ic % 2
            E.op("tensor", lambda e, fp=fp, ic=ic, gb=gb: [
                 e.matmul(sc_ps[gb][:, 0:512],
                          wgT[:, DI * hc + 128 * ic:DI * hc + 128 * ic + 128],
                          xq_sb[fp][:, 512 * hc:512 * (hc + 1)],
                          start=(hc == 0), stop=(hc == NC - 1)) for hc in range(NC)][-1],
                 waits=[("xqa", xqdv), ("wt", wt_gu), kq_free[gb]], inc="gu")
            gu_g = E.now("gu")
            E.op("tensor", lambda e, fp=fp, ic=ic, gb=gb: [
                 e.matmul(po_ps[gb][:, 0:512],
                          wuT[:, DI * hc + 128 * ic:DI * hc + 128 * ic + 128],
                          xq_sb[fp][:, 512 * hc:512 * (hc + 1)],
                          start=(hc == 0), stop=(hc == NC - 1)) for hc in range(NC)][-1],
                 waits=[po_free[gb]], inc="gu")
            E.op("scalar", lambda e, gb=gb:
                 e.activation(out=g_sb[gb][:], in_=sc_ps[gb][:, 0:512], func=AF.Silu),
                 waits=[("gu", gu_g), g_free[gb]], inc="sil")
            kq_free[gb] = ("sil", E.now("sil"))

            slot = (f * NI + ic) % 4
            E.op("vector", lambda e, gb=gb, slot=slot:
                 e.tensor_tensor(t_sl[slot][:], po_ps[gb][:, 0:512], g_sb[gb][:], ALU.mult),
                 waits=[("sil", E.now("sil")), ("gu", E.now("gu")), t_free[slot]], inc="tt")
            po_free[gb] = ("tt", E.now("tt"))
            g_free[gb] = ("tt", E.now("tt"))
            E.op("vector", lambda e, slot=slot:
                 e.reduce_max(tstat[:, 0:1], t_sl[slot][:], AX.X, apply_absolute_value=True),
                 waits=[("tt", E.now("tt")), ("ini", 1)], inc="tt")
            E.op("vector", lambda e: e.tensor_max(tmax_run[:], tmax_run[:], tstat[:, 0:1]),
                 waits=[("tt", E.now("tt"))], inc="tt")
            tsts = "tst%d" % slot
            E.op("gpsimd", lambda e, f=f, ic=ic, slot=slot:
                 e.dma_start(out=t_sc[ic, :, 512 * f:512 * (f + 1)], in_=t_sl[slot][:]),
                 waits=[("tt", E.now("tt"))], inc=tsts, k=16)
            t_free[slot] = (tsts, E.now(tsts))
        xq_free[fp] = ("gu", E.now("gu"))

    E.op("tensor", lambda e: e.transpose(sc_ps[0][0:1, 0:128], tmax_run[:], idn_sb[:]),
         waits=[("tt", E.now("tt")), kq_free[0]], inc="tp2")
    E.op("vector", lambda e: e.reduce_max(gstat[0:1, 0:1], sc_ps[0][0:1, 0:128], AX.X),
         waits=[("tp2", E.now("tp2")), ("arx", 32)], inc="sch")
    kq_free[0] = ("sch", E.now("sch"))
    E.op("sync", lambda e: e.dma_start(out=ar2_in[:], in_=gstat[:]),
         waits=[("sch", E.now("sch"))], inc="arx", k=16)
    E.op("gpsimd", lambda e: e.collective_compute(
        "AllReduce", ALU.max, replica_groups=[[0, 1, 2, 3, 4, 5, 6, 7]],
        ins=[ar2_in[:]], outs=[ar2_out[:]]),
        waits=[("arx", E.now("arx"))], inc="cc2")

    E.op("sync", lambda e, sh: [e.dma_start(out=wdT[:, 1024 * ic:1024 * (ic + 1)],
                                        in_=wdq_sc[:, 128 * ic:128 * (ic + 1)], transpose=True).then_inc(sh, 16)
                            for ic in range(NI)][-1],
         waits=[("wstd", 16 * NC), ("gu", E.now("gu")), ("tt", E.now("tt")),
                ("xqd", E.now("xqd")), ("xqa", E.now("xqa")), ("sil", E.now("sil"))]
               + [("tst%d" % i_, E.now("tst%d" % i_)) for i_ in range(4)],
         inc="wt", k=16 * NI, self_inc=True)
    wt_d = E.now("wt")

    sh_ready = scale_chain(ar2_out, cd_sb, sh_bc, rh_bc, "cc2", E.now("cc2"))

    GU_ALL = E.now("gu")
    TT_ALL = E.now("tt")
    TST_ALL = [("tst%d" % i_, E.now("tst%d" % i_)) for i_ in range(4)]
    SIL_ALL = E.now("sil")
    tB_free = [("gu", GU_ALL), ("gu", GU_ALL)]
    hB_free = [("gu", GU_ALL), ("gu", GU_ALL)]
    tBi_first = [True]
    xB_free = [("gu", GU_ALL), ("gu", GU_ALL)]
    aoB_free = [TST_ALL[0], TST_ALL[1], TST_ALL[2], TST_ALL[3]]
    outB_free = [TST_ALL[2], TST_ALL[3]]
    dn_banks = [kq_free[0], kq_free[1], po_free[0], po_free[1]]

    fb_pend = {}
    for step in range(NF2 + 1):
        if step < NF2:
            f2 = step
            fp = f2 % 2
            E.op("sync", lambda e, f2=f2, fp=fp:
                 e.dma_start(out=tB[fp][:],
                             in_=t_sc.rearrange("i p s -> p i s")[:, :, 256 * f2:256 * (f2 + 1)]),
                 waits=TST_ALL + [tB_free[fp]], inc="tld%d" % fp, k=16)
            w0 = [xB_free[fp]]
            if f2 < 2:
                w0 += [("ope", P2_OPE), ("qpe", P2_QPE)]
            E.op("sync", lambda e, f2=f2, fp=fp:
                 e.dma_start(out=xB[fp][:], in_=xm[256 * f2:256 * (f2 + 1), :]
                             .rearrange("(k p) d -> p k d", p=128)),
                 waits=w0, inc="xbl%d" % fp, k=16)
            xblv = ("xbl%d" % fp, E.now("xbl%d" % fp))
            aotv = []
            for g in range(2):
                sl4 = 2 * fp + g
                E.op("sync", lambda e, f2=f2, g=g, sl4=sl4:
                     e.dma_start(out=aoB[sl4][:], in_=ao_sc[:, 128 * (2 * f2 + g):128 * (2 * f2 + g + 1)],
                                 transpose=True),
                     waits=P2_AOS + [aoB_free[sl4]], inc="aot%d" % sl4, k=16)
                aotv.append(("aot%d" % sl4, E.now("aot%d" % sl4)))

            QW = NI * 256 // 4
            tldw = ("tld%d" % fp, E.now("tld%d" % fp))
            for qw in range(4):
                sl = slice(QW * qw, QW * (qw + 1))
                w0 = [tldw, ("sch", sh_ready)] if qw == 0 else [tldw]
                if tBi_first[0]:
                    w0 += [("sil", SIL_ALL), ("gu", GU_ALL)]
                    tBi_first[0] = False
                E.op("scalar", lambda e, fp=fp, sl=sl:
                     e.activation(out=tBi[:, sl], in_=tB[fp][:, sl], func=AF.Identity,
                                  scale=rh_bc[:, 0:1]),
                     waits=w0, inc="hqa")
                E.op("vector", lambda e, sl=sl:
                     e.tensor_scalar(tBi[:, sl], tBi[:, sl], -QMAX - 1.0, QMAX, ALU.max, ALU.min),
                     waits=[("hqa", E.now("hqa"))], inc="hq")
                w1 = [("hq", E.now("hq"))]
                if qw == 0:
                    w1.append(hB_free[fp])
                E.op("scalar", lambda e, fp=fp, sl=sl:
                     e.activation(out=hB[fp][:, sl], in_=tBi[:, sl], func=AF.Identity,
                                  scale=sh_bc[:, 0:1]),
                     waits=w1, inc="hqa")
            hqv = E.now("hqa")
            tB_free[fp] = ("hqa", hqv)
            fb_pend[f2] = (fp, hqv, xblv, aotv)

        if step >= 1:
            f2 = step - 1
            fp, hqv, xblv, aotv = fb_pend.pop(f2)
            for g in range(2):
                for hb in range(2):
                    bidx = (2 * g + hb)
                    bank = [sc_ps[0], sc_ps[1], po_ps[0], po_ps[1]][bidx]
                    E.op("tensor", lambda e, fp=fp, g=g, hb=hb, bank=bank: [
                         e.matmul(bank[:, 0:512],
                                  hB[fp][:, 256 * ic + 128 * g:256 * ic + 128 * g + 128],
                                  wdT[:, 1024 * ic + 512 * hb:1024 * ic + 512 * hb + 512],
                                  start=(ic == 0), stop=(ic == NI - 1)) for ic in range(NI)][-1],
                         waits=[("hqa", hqv), ("wt", wt_d), dn_banks[bidx]], inc="dn")
                    wv_ = [("dn", E.now("dn")), xblv, aotv[g]]
                    if hb == 0:
                        wv_.append(outB_free[g])
                    E.op("vector", lambda e, fp=fp, g=g, hb=hb, bank=bank:
                         e.tensor_add(outB[g][:, 512 * hb:512 * (hb + 1)], bank[:, 0:512],
                                      xB[fp][:, 1024 * g + 512 * hb:1024 * g + 512 * hb + 512]),
                         waits=wv_, inc="out")
                    E.op("vector", lambda e, fp=fp, g=g, hb=hb:
                         e.tensor_add(outB[g][:, 512 * hb:512 * (hb + 1)],
                                      outB[g][:, 512 * hb:512 * (hb + 1)],
                                      aoB[2 * fp + g][:, 512 * hb:512 * (hb + 1)]),
                         waits=[("out", E.now("out"))], inc="out")
                    dn_banks[bidx] = ("out", E.now("out"))
                E.op("sync", lambda e, f2=f2, g=g:
                     e.dma_start(out=y_h[128 * (2 * f2 + g):128 * (2 * f2 + g + 1), :], in_=outB[g][:]),
                     waits=[("out", E.now("out"))], inc="yst%d" % g, k=16)
                outB_free[g] = ("yst%d" % g, E.now("yst%d" % g))
                aoB_free[2 * fp + g] = ("out", E.now("out"))
            hB_free[fp] = ("dn", E.now("dn"))
            xB_free[fp] = ("out", E.now("out"))

    E.op("sync", lambda e: None,
         waits=[("yst0", E.now("yst0")), ("yst1", E.now("yst1"))])

    with nc.Block() as block:
        @block.sync
        def _(e):
            E.replay("sync", e)

        @block.gpsimd
        def _(e):
            E.replay("gpsimd", e)

        @block.scalar
        def _(e):
            E.replay("scalar", e)

        @block.vector
        def _(e):
            E.replay("vector", e)

        @block.tensor
        def _(e):
            E.replay("tensor", e)

    return nc


def _make_masks(h, S):
    m = np.ones((5, 128, 256), dtype=np.float32)
    s_loc = np.arange(128)[:, None]
    for r in range(5):
        for half, qoff in ((0, h), (1, 2 + h)):
            q_loc = np.arange(128)[None, :]
            masked = (r - qoff) * 128 + s_loc <= q_loc + 1
            m[r][:, 128 * half:128 * (half + 1)][masked] = 0.0
    return m.reshape(5 * 128, 256).astype(ml_dtypes.bfloat16)


def _make_fixes(h, S):
    fm = np.ones((128, 256), dtype=np.float32)
    fa = np.zeros((128, 256), dtype=np.float32)
    fl = np.zeros((1, 256), dtype=np.float32)
    if h == 1:
        fm[:, 254:256] = 0.0
        fa[:, 254:256] = 1.0
        fl[0, 254:256] = 1.0
    fixes = np.concatenate([fm, fa], axis=1).astype(ml_dtypes.bfloat16)
    return fixes, fl


_NC_CACHE = {}


def _get_nc(S, D, DI):
    key = (S, D, DI)
    if key not in _NC_CACHE:
        _NC_CACHE[key] = build_nc(S, D, DI)
    return _NC_CACHE[key]


def kernel(L0_attn_x, L0_attn_wq, L0_attn_bq, L0_attn_wk, L0_attn_bk,
           L0_attn_wv, L0_attn_bv, L0_attn_wo, L0_attn_bo,
           L0_ffn_clip_x, L0_ffn_clip_down, L0_ffn_w_gate, L0_ffn_w_up,
           L0_ffn_w_down, _S=None):
    x = np.ascontiguousarray(np.asarray(L0_attn_x, dtype=np.float32))
    B, S, D = x.shape
    DI = int(np.asarray(L0_ffn_w_gate).shape[0])
    nc = _get_nc(S, D, DI)

    f32 = lambda a: np.ascontiguousarray(np.asarray(a, dtype=np.float32))
    wq, wk, wv, wo = f32(L0_attn_wq), f32(L0_attn_wk), f32(L0_attn_wv), f32(L0_attn_wo)
    bq, bk, bv, bo = (f32(L0_attn_bq).reshape(1, D), f32(L0_attn_bk).reshape(1, D),
                      f32(L0_attn_bv).reshape(1, D), f32(L0_attn_bo).reshape(1, D))
    wg, wu, wd = f32(L0_ffn_w_gate), f32(L0_ffn_w_up), f32(L0_ffn_w_down)
    cx = f32(L0_ffn_clip_x).reshape(1, 1)
    cd = f32(L0_ffn_clip_down).reshape(1, 1)
    ident = np.eye(128, dtype=np.float32)

    in_maps = []
    for core in range(8):
        b, h = core // 2, core % 2
        xb = x[b]
        x_mine = np.ascontiguousarray(xb.reshape(S // 128, 128, D)[h::2].reshape(S // 2, D))
        fixes, fl = _make_fixes(h, S)
        in_maps.append({
            "x_true": xb, "x_true_fm": np.ascontiguousarray(xb.T), "x_mine": x_mine,
            "wq": wq, "wk": wk, "wv": wv, "wo": wo,
            "bq": bq, "bk": bk, "bv": bv, "bo": bo,
            "clip_x": cx, "clip_down": cd,
            "w_gate": wg, "w_up": wu, "w_down": wd,
            "masks": _make_masks(h, S), "fixes": fixes, "fix_l": fl,
            "ident": ident,
        })

    res = run_bass_kernel_spmd(nc, in_maps, core_ids=list(range(8)))
    y = np.empty((B, S, D), dtype=np.float32)
    for core in range(8):
        b, h = core // 2, core % 2
        yv = y[b].reshape(S // 128, 128, D)
        yv[h::2] = np.asarray(res.results[core]["y"]).reshape(S // 256, 128, D)
    return y
